# version 24
# baseline (speedup 1.0000x reference)
import numpy as np
import ml_dtypes

import concourse.bass as bass
import concourse.mybir as mybir
from concourse.bass_utils import run_bass_kernel_spmd

F32 = mybir.dt.float32
BF16 = mybir.dt.bfloat16
AF = mybir.ActivationFunctionType
ALU = mybir.AluOpType
AX = mybir.AxisListType
BF16NP = ml_dtypes.bfloat16

H = 16
DH = 64
E = 1024
CH = 128
MC = 4
MAC = CH * MC
EPS = 1e-5
N_CORES = 8
STAGE = 99
DBG = None


def _consts():
    c = {}
    c["c_ident"] = np.eye(128, dtype=BF16NP)
    s = np.arange(128)[:, None]
    t = np.arange(128)[None, :]
    c["c_mask"] = (s <= t).astype(BF16NP)
    rcolA = np.zeros((32, 512), np.float32)
    for b in range(32):
        rcolA[b, b * 16:(b + 1) * 16] = 1.0
    c["c_rcolA"] = rcolA.astype(BF16NP)
    rrA = np.zeros((32, 4, 128), np.float32)
    for p in range(4):
        for dg in range(128):
            rrA[8 * p + dg // 16, p, dg] = 1.0
    c["c_rrowA"] = rrA.reshape(32, 512).astype(BF16NP)
    rcolB = np.zeros((16, 512), np.float32)
    for b in range(32):
        for e in range(16):
            rcolB[e, b * 16 + e] = 1.0
    c["c_rcolB"] = rcolB.astype(BF16NP)
    rrB = np.zeros((16, 128), np.float32)
    for dg in range(128):
        rrB[dg % 16, dg] = 1.0
    c["c_rrowB"] = rrB.astype(BF16NP)
    c["c_ones2"] = np.ones((2, 128), BF16NP)
    c["c_onescol"] = np.ones((128, 1), BF16NP)
    return c


class Ctx:

    ENG = {"pe": "tensor", "dve": "vector", "act": "scalar", "sync": "sync", "gp": "gpsimd"}

    def __init__(self, nc, block, sems):
        self.nc = nc
        self.block = block
        self.sem = sems
        self.cnt = {"pe": 0, "dve": 0, "act": 0}
        self.dmac = {"wg": 0, **{f"w{i}": 0 for i in range(16)},
                     "ld0": 0, "ld1": 0, "ld2": 0, "ld3": 0, "ld4": 0, "ld5": 0, "ld6": 0,
                     "st0": 0, "st1": 0}
        self.stage = {k: [] for k in self.ENG}

    def wait(self, eng, evt):
        if evt is None:
            return
        sname, val = evt
        if val <= 0:
            return
        if sname in self.cnt:
            assert val <= self.cnt[sname], f"wait on future evt {sname}={val} (cur {self.cnt[sname]})"
        else:
            assert val <= self.dmac[sname], f"wait on future dma {sname}={val}"
        sem = self.sem[sname]
        self.stage[eng].append(lambda e, sem=sem, val=val: e.wait_ge(sem, val))

    def wait_all(self, eng, evts):
        merged = {}
        for evt in evts:
            if evt is None:
                continue
            s, v = evt
            merged[s] = max(merged.get(s, 0), v)
        for s, v in merged.items():
            self.wait(eng, (s, v))

    def pe(self, f, sig=False):
        if sig:
            self.cnt["pe"] += 1
            n = self.cnt["pe"]
            sem = self.sem["pe"]
            self.stage["pe"].append(lambda e, f=f: f(e).then_inc(sem, 1))
            return ("pe", n)
        self.stage["pe"].append(lambda e, f=f: f(e))
        return None

    def _chained(self, eng, f):
        n0 = self.cnt[eng]
        self.cnt[eng] = n0 + 1
        sem = self.sem[eng]

        def run(e, f=f, n0=n0, sem=sem):
            inst = f(e)
            if n0 > 0:
                inst._wait_ge(sem, n0)
            inst.then_inc(sem, 1)

        self.stage[eng].append(run)
        return (eng, n0 + 1)

    def dve(self, f):
        return self._chained("dve", f)

    def act(self, f):
        return self._chained("act", f)

    def dma(self, eng, ring, f):
        self.dmac[ring] += 16
        v = self.dmac[ring]
        sem = self.sem[ring]
        self.stage[eng].append(lambda e, f=f, sem=sem: f(e).then_inc(sem, 16))
        return (ring, v)

    def flush(self):
        for eng, lst in self.stage.items():
            if not lst:
                continue

            def runner(e, lst=tuple(lst)):
                for t in lst:
                    t(e)

            getattr(self.block, self.ENG[eng])(runner)
        self.stage = {k: [] for k in self.ENG}


class Guard:

    def __init__(self, ctx):
        self.ctx = ctx
        self.w = []
        self.r = []

    def before_write(self, eng):
        self.ctx.wait_all(eng, self.r if self.r else self.w)

    def wrote(self, *evts):
        self.w = [e for e in evts if e is not None]
        self.r = []

    def before_read(self, eng):
        self.ctx.wait_all(eng, self.w)

    def read(self, *evts):
        self.r += [e for e in evts if e is not None]


def build(n_seq, n_mac):
    nc = bass.Bass("TRN2", target_bir_lowering=False)
    L = n_mac * MAC

    x_d = nc.dram_tensor("L0_attn_x", [n_seq, L, E], F32, kind="ExternalInput").ap()
    y_d = nc.dram_tensor("y", [n_seq, L, E], F32, kind="ExternalOutput").ap()
    w_d = {k: nc.dram_tensor(f"L0_attn_{k}", [E, E], F32, kind="ExternalInput").ap()
           for k in ("Wq", "Wk", "Wv", "Wo")}
    bo_d = nc.dram_tensor("L0_attn_bo", [E], F32, kind="ExternalInput").ap()
    g_d = {k: nc.dram_tensor(f"L0_ffn_{k}", [1, 32, 4, 32] if k.startswith("G1") else [4, 16, 1, 16],
                             F32, kind="ExternalInput").ap()
           for k in ("G1_R", "G2_R", "G1_I", "G2_I")}
    br_d = nc.dram_tensor("L0_ffn_bias_real", [512], F32, kind="ExternalInput").ap()
    bi_d = nc.dram_tensor("L0_ffn_bias_imag", [512], F32, kind="ExternalInput").ap()
    cd = {}
    for k, arr in _consts().items():
        cd[k] = nc.dram_tensor(k, list(arr.shape), BF16 if arr.dtype == BF16NP else F32,
                               kind="ExternalInput").ap()

    import contextlib
    ctxmgr = contextlib.ExitStack()
    sb = lambda name, shape, dt: ctxmgr.enter_context(nc.sbuf_tensor(name, shape, dt))
    ps = lambda name, shape, dt: ctxmgr.enter_context(nc.psum_tensor(name, shape, dt))

    with ctxmgr:
        wt = {k: sb(f"wt_{k}", [128, 8, 1024], BF16) for k in ("Wq", "Wk", "Wv", "Wo")}
        x_ch = [sb(f"x{i}", [128, 1024], F32) for i in range(7)]
        xn_tok = sb("xn_tok", [128, 1024], BF16)
        xn_pre = sb("xn_pre", [128, 4, 1024], BF16)
        xn_fm = sb("xn_fm", [128, 8, MAC], BF16)
        phi_q = sb("phi_q", [128, 8, MAC], BF16)
        phi_k = sb("phi_k", [128, 8, MAC], BF16)
        pk_tok = [sb(f"pk_tok{i}", [128, 1024], BF16) for i in range(2)]
        v_sb = [sb(f"v_sb{i}", [128, 16 * 66], BF16) for i in range(2)]
        sc_sb = [sb(f"sc_sb{i}", [128, 2048], BF16) for i in range(2)]
        att_tok = [sb(f"att_tok{i}", [128, 1024], BF16) for i in range(2)]
        att_fm = sb("att_fm", [128, 8, 128], BF16)
        out_sb = [sb(f"out{i}", [128, 1024], F32) for i in range(2)]
        zb = sb("zb", [128, 8], F32)
        s_bf = [sb(f"s_bf{i}", [128, 16 * 66], BF16) for i in range(2)]
        den_t = sb("den_t", [128, 4], F32)
        rec_t = sb("rec_t", [128, 4], F32)
        st6 = [sb(f"st6_{i}", [128, 12], F32) for i in range(2)]
        mv4 = [sb(f"mv4_{i}", [128, 4, 2], F32) for i in range(2)]
        lnv4 = sb("lnv4", [128, 4], F32)
        rstd4 = [sb(f"rstd4_{i}", [128, 4], F32) for i in range(2)]
        nmr4 = [sb(f"nmr4_{i}", [128, 4], F32) for i in range(2)]
        epsc = sb("epsc", [128, 1], F32)
        ident = sb("ident", [128, 128], BF16)
        mask = sb("mask", [128, 128], BF16)
        rcA = sb("rcA", [32, 512], BF16)
        rrA = sb("rrA", [32, 512], BF16)
        rcB = sb("rcB", [16, 512], BF16)
        rrB = sb("rrB", [16, 128], BF16)
        ones2 = sb("ones2", [2, 128], BF16)
        onescol = sb("onescol", [128, 1], BF16)
        wstage = [sb(f"wstg{i}", [128, 8, 1024], BF16) for i in range(2)]
        m_sb = wstage[0]
        g1_sb = {p: sb(f"g1{p}", [32, 128], BF16) for p in "RI"}
        g2_sb = {p: sb(f"g2{p}", [16, 64], BF16) for p in "RI"}

        bias_hi = sb("bias_hi", [1, 1024], BF16)
        bias_lo = sb("bias_lo", [1, 1024], BF16)
        bP = [ps("bP0", [128, 512], F32), ps("bP1", [128, 512], F32)]
        bS = [ps("bS0", [128, 512], F32), ps("bS1", [128, 512], F32)]
        bN = [ps("bN0", [128, 512], F32), ps("bN1", [128, 512], F32)]
        bT = ps("bT", [128, 1024], BF16)
        bSD = ps("bSD", [128, 512], F32)

        import contextlib as _ctl
        semstack = ctxmgr
        semnames = ["pe", "dve", "act", "wg"] + [f"w{i}" for i in range(16)] + [
                    "ld0", "ld1", "ld2", "ld3", "ld4", "ld5", "ld6", "st0", "st1"]
        sems = {n: ctxmgr.enter_context(nc.semaphore(f"s_{n}")) for n in semnames}
        with nc.Block() as block:
            K = Ctx(nc, block, sems)
            gP = [Guard(K), Guard(K)]
            gS = [Guard(K), Guard(K)]
            gN = [Guard(K), Guard(K)]
            gT = Guard(K)
            g_xbuf = [Guard(K) for _ in range(7)]
            g_obuf = [Guard(K), Guard(K)]

            g_stats = [Guard(K), Guard(K)]
            g_xntok = Guard(K)
            g_xnpre = Guard(K)
            g_xnfm = Guard(K)
            g_phi = Guard(K)
            g_pktok = [Guard(K), Guard(K)]
            g_vsb = [Guard(K), Guard(K)]
            g_scsb = [[Guard(K) for _ in range(4)] for _ in range(2)]
            g_atok = [Guard(K), Guard(K)]
            g_afm = Guard(K)
            gSD = Guard(K)
            g_sbf = [Guard(K), Guard(K)]
            g_zb = Guard(K)
            bo_sb = out_sb[0]
            bcat = out_sb[1]
            tmp_f = out_sb[0]
            def accR_ap(p0):
                return x_ch[3 + p0 // 2][:, 512 * (p0 % 2):512 * (p0 % 2) + 512]

            def accI_ap(p0):
                return x_ch[5 + p0 // 2][:, 512 * (p0 % 2):512 * (p0 % 2) + 512]

            for dst, src in ((ident, cd["c_ident"]), (mask, cd["c_mask"]),
                             (rcA, cd["c_rcolA"]), (rrA, cd["c_rrowA"]),
                             (rcB, cd["c_rcolB"]), (rrB, cd["c_rrowB"]),
                             (ones2, cd["c_ones2"]), (onescol, cd["c_onescol"])):
                K.dma("gp", "wg", lambda e, d=dst, s=src: e.dma_start(d[:, :], s))
            K.dma("gp", "wg", lambda e: e.dma_start(bo_sb[0:1, 0:1024], bo_d.rearrange("(o k) -> o k", o=1)))
            K.dma("gp", "wg", lambda e: e.dma_start(bcat[0:1, 0:512], br_d.rearrange("(o k) -> o k", o=1)))
            K.dma("gp", "wg", lambda e: e.dma_start(bcat[0:1, 512:1024], bi_d.rearrange("(o k) -> o k", o=1)))
            for p in "RI":
                K.dma("gp", "wg",
                      lambda e, p=p: e.dma_start(g1_sb[p][:, :], g_d[f"G1_{p}"].rearrange("o b c d -> (o b) (c d)")))
                K.dma("gp", "wg",
                      lambda e, p=p: e.dma_start(g2_sb[p][:, :].rearrange("p (c g) -> p c g", g=16),
                                                 g_d[f"G2_{p}"].rearrange("c e o g -> (e o) c g")))
            small_done = ("wg", K.dmac["wg"])
            w_evt = {}

            def load_weight(k, idx, war_evt=None):
                K.wait("gp", war_evt)
                evs = []
                for q in range(4):
                    ring = f"w{4 * idx + q}"
                    K.dma("gp", ring,
                          lambda e, k=k, idx=idx, q=q: e.dma_start(
                              wstage[idx % 2][:, 2 * q:2 * q + 2, :],
                              w_d[k].rearrange("(r p) k -> p r k", p=128)[:, 2 * q:2 * q + 2, :]))
                    evs.append((ring, K.dmac[ring]))
                w_evt[k] = evs

            load_weight("Wq", 0)
            load_weight("Wk", 1)

            NCH = n_seq * n_mac * MC
            x_evt = {}

            def load_chunk(j):
                if j >= NCH:
                    return
                s_i = j // (n_mac * MC)
                row = (j % (n_mac * MC)) * CH
                ring = f"ld{j % 7}"
                g_xbuf[j % 7].before_write("sync")
                K.dma("sync", ring,
                      lambda e, s_i=s_i, row=row: e.dma_start(
                          x_ch[j % 7][:, :], x_d[s_i, row:row + CH, :]))
                x_evt[j] = (ring, K.dmac[ring])
                g_xbuf[j % 7].wrote(x_evt[j])

            K.wait("dve", small_done)
            for _vb in v_sb:
                K.dve(lambda e, _vb=_vb: e.memset(_vb[:, :].rearrange("p (h w) -> p h w", w=66)[:, :, 64:65], 1.0))
                K.dve(lambda e, _vb=_vb: e.memset(_vb[:, :].rearrange("p (h w) -> p h w", w=66)[:, :, 65:66], 0.0))
            K.dve(lambda e: e.memset(zb[:, :], 0.0))
            for _sbf in s_bf:
                K.dve(lambda e, _sbf=_sbf: e.memset(_sbf[:, :], 0.0))
            K.dve(lambda e: e.memset(epsc[:, :], EPS))
            evb1 = K.dve(lambda e: e.tensor_add(bcat[0:1, 0:512], bcat[0:1, 0:512], bo_sb[0:1, 0:512]))
            evb2 = K.dve(lambda e: e.tensor_add(bcat[0:1, 512:1024], bcat[0:1, 512:1024], bo_sb[0:1, 512:1024]))
            evb3 = K.dve(lambda e: e.tensor_copy(bias_hi[0:1, :], bcat[0:1, 0:1024]))
            evb4 = K.dve(lambda e: e.tensor_sub(bias_lo[0:1, :], bcat[0:1, 0:1024], bias_hi[0:1, :]))
            bias_ready = evb4


            wt_evt = {}

            wprep_pe = {}

            def weight_prep(k, idx):
                stg = wstage[idx % 2]
                K.wait_all("pe", w_evt[k])
                last_evac = None
                for r in range(8):
                    gT.before_write("pe")
                    for kk in range(8):
                        K.pe(lambda e, r=r, kk=kk, stg=stg: e.transpose(
                            bT[:, 128 * kk:128 * (kk + 1)], stg[:, r, 128 * kk:128 * (kk + 1)], ident[:, :]),
                            sig=(kk == 7))
                    evt_t = ("pe", K.cnt["pe"])
                    gT.wrote(evt_t)
                    K.wait("act", evt_t)
                    ev = K.act(lambda e, r=r, k=k: e.copy(
                        wt[k][:, :, 128 * r:128 * (r + 1)],
                        bT[:, :].rearrange("p (kk t) -> p kk t", t=128)))
                    gT.read(ev)
                    last_evac = ev
                wprep_pe[k] = ("pe", K.cnt["pe"])
                wt_evt[k] = last_evac

            K.wait("pe", small_done)
            for j in range(3):
                load_chunk(j)
            weight_prep("Wq", 0)
            load_weight("Wv", 2, war_evt=wprep_pe["Wq"])
            weight_prep("Wk", 1)
            load_weight("Wo", 3, war_evt=wprep_pe["Wk"])
            weight_prep("Wv", 2)
            weight_prep("Wo", 3)

            K.wait("pe", small_done)
            acc = {"R": accR_ap, "I": accI_ap}
            accev = {}
            g_t1s, g_t2s = Guard(K), Guard(K)
            t1s = xn_tok[0:32, 0:512]
            t2s = xn_tok[0:16, 512:1024]
            g_aS = g_afm
            aS = att_fm[:, :, :].rearrange("p a b -> p (a b)")[:, 0:512]
            for p in "RI":
                for c in range(4):
                    gS[0].before_write("pe")
                    e1 = K.pe(lambda e, p=p, c=c: e.matmul(
                        bS[0][0:32, :], g1_sb[p][0:32, 32 * c:32 * (c + 1)], rcA[0:32, :],
                        start=True, stop=True), sig=True)
                    gS[0].wrote(e1)
                    gS[0].before_read("act")
                    g_t1s.before_write("act")
                    ev1 = K.act(lambda e: e.copy(t1s[0:32, :], bS[0][0:32, :]))
                    g_t1s.wrote(ev1)
                    gS[0].read(ev1)
                    gS[1].before_write("pe")
                    e2 = K.pe(lambda e, p=p, c=c: e.matmul(
                        bS[1][0:16, :], g2_sb[p][0:16, 16 * c:16 * (c + 1)], rcB[0:16, :],
                        start=True, stop=True), sig=True)
                    gS[1].wrote(e2)
                    gS[1].before_read("act")
                    g_t2s.before_write("act")
                    ev2 = K.act(lambda e: e.copy(t2s[0:16, :], bS[1][0:16, :]))
                    g_t2s.wrote(ev2)
                    gS[1].read(ev2)
                    for p0 in range(4):
                        gN[0].before_write("pe")
                        K.wait("pe", ev1)
                        eA = K.pe(lambda e, p0=p0: e.matmul(
                            bN[0][:, :], rrA[0:32, 128 * p0:128 * (p0 + 1)], t1s[0:32, :],
                            start=True, stop=True), sig=True)
                        gN[0].wrote(eA)
                        gSD.before_write("pe")
                        K.wait("pe", ev2)
                        eB = K.pe(lambda e: e.matmul(
                            bSD[:, :], rrB[0:16, :], t2s[0:16, :],
                            start=True, stop=True), sig=True)
                        gSD.wrote(eB)
                        gN[0].before_read("act")
                        g_aS.before_write("act")
                        evA = K.act(lambda e: e.copy(aS[:, :], bN[0][:, :]))
                        g_aS.wrote(evA)
                        gN[0].read(evA)
                        gSD.before_read("dve")
                        K.wait("dve", evA)
                        if c == 0:
                            evP = K.dve(lambda e, p=p, p0=p0: e.tensor_mul(
                                acc[p](p0), bSD[:, :], aS[:, :]))
                        else:
                            K.dve(lambda e: e.tensor_mul(tmp_f[:, 0:512], bSD[:, :], aS[:, :]))
                            evP = K.dve(lambda e, p=p, p0=p0: e.tensor_add(
                                acc[p](p0), acc[p](p0), tmp_f[:, 0:512]))
                        gSD.read(evP)
                        g_aS.read(evP)
                        g_t1s.read(eA)
                        g_t2s.read(eB)
                accev[p] = evP
            m_parts = []
            K.wait("dve", wprep_pe["Wv"])
            for p0 in range(4):
                m_parts.append(K.dve(lambda e, p0=p0: e.tensor_copy(m_sb[:, p0, 0:512], accR_ap(p0))))
                m_parts.append(K.dve(lambda e, p0=p0: e.tensor_copy(m_sb[:, p0, 512:1024], accI_ap(p0))))
                m_parts.append(K.dve(lambda e, p0=p0: e.tensor_scalar(
                    m_sb[:, 4 + p0, 0:512], accI_ap(p0), -1.0, None, ALU.mult)))
                m_parts.append(K.dve(lambda e, p0=p0: e.tensor_copy(m_sb[:, 4 + p0, 512:1024], accR_ap(p0))))
            m_ready = m_parts[-1]

            K.wait("sync", m_ready)
            for j in range(3, 7):
                load_chunk(j)
            K.flush()

            chs = {}
            CPS = n_mac * MC

            pre_nmr = {}

            def emit_stats_chunk(mm, cc):
                s = mm % 2
                j = mm * MC + cc
                K.wait("dve", x_evt[j])
                K.dve(lambda e, j=j, s=s: e.bn_stats(st6[s][:, 0:6], x_ch[j % 7][:, 0:512]))
                K.dve(lambda e, j=j, s=s: e.bn_stats(st6[s][:, 6:12], x_ch[j % 7][:, 512:1024]))
                ev = K.dve(lambda e, cc=cc, s=s: e.bn_aggr(mv4[s][:, cc, 0:2], st6[s][:, :]))
                g_xbuf[j % 7].read(ev)

            def emit_rstd(mm):
                s = mm % 2
                ev_mv = ("dve", K.cnt["dve"])
                K.wait("act", ev_mv)
                K.act(lambda e, s=s: e.activation(lnv4[:, 0:4], mv4[s][:, :, 1], AF.Ln,
                                                  bias=epsc[:, 0:1], scale=1.0))
                ev_rstd = K.act(lambda e, s=s: e.activation(rstd4[s][:, 0:4], lnv4[:, 0:4], AF.Exp,
                                                            bias=0.0, scale=-0.5))
                K.wait("dve", ev_rstd)
                ev_nmr = K.dve(lambda e, s=s: e.scalar_tensor_tensor(
                    nmr4[s][:, 0:4], mv4[s][:, :, 0], -1.0, rstd4[s][:, 0:4], ALU.mult, ALU.mult))
                g_stats[s].wrote(ev_nmr, ev_rstd)
                pre_nmr[mm] = ev_nmr
                return ev_nmr

            nrm_ev = {}

            def emit_normalize(m, cc):
                s = m % 2
                j = m * MC + cc
                if cc == 0:
                    g_xnpre.before_write("act")
                    K.wait("act", pre_nmr[m])
                ev_xn = K.act(lambda e, j=j, cc=cc, s=s: e.activation(
                    xn_pre[:, cc, :], x_ch[j % 7][:, :], AF.Identity,
                    bias=nmr4[s][:, cc:cc + 1], scale=rstd4[s][:, cc:cc + 1]))
                g_stats[s].read(ev_xn)
                g_xbuf[j % 7].read(ev_xn)
                nrm_ev[(m, cc)] = ev_xn
                if cc == MC - 1:
                    g_xnpre.wrote(*[nrm_ev[(m, k)] for k in range(MC)])

            def emit_xnT(m, cc, ev_nmr):
                gT.before_write("pe")
                K.wait("pe", nrm_ev[(m, cc)])
                for ic in range(8):
                    K.pe(lambda e, ic=ic, cc=cc: e.transpose(
                        bT[:, 128 * ic:128 * (ic + 1)], xn_pre[:, cc, 128 * ic:128 * (ic + 1)], ident[:, :]),
                        sig=(ic == 7))
                ev_t = ("pe", K.cnt["pe"])
                gT.wrote(ev_t)
                g_xnpre.read(ev_t)
                gT.before_read("dve")
                if cc == 0:
                    g_xnfm.before_write("dve")
                ev_evac = K.dve(lambda e, cc=cc: e.tensor_copy(
                    xn_fm[:, :, 128 * cc:128 * (cc + 1)],
                    bT[:, :].rearrange("p (kk t) -> p kk t", t=128)))
                gT.read(ev_evac)
                return ev_evac

            def emit_qk(m, xn_evts):
                g_xnfm.wrote(*xn_evts)
                if m == 0:
                    K.wait("pe", wt_evt["Wq"])
                qk_banks = [bP[0], bP[1], bN[0], bN[1]]
                qk_guards = [gP[0], gP[1], gN[0], gN[1]]
                esc_slots = [xn_tok[:, 0:512], xn_tok[:, 512:1024],
                             att_tok[0][:, 0:512], att_tok[0][:, 512:1024]]
                esc_guards = [g_xntok, g_xntok, g_atok[0], g_atok[0]]
                for pi, (wkey, phi_dst) in enumerate((("Wq", phi_q), ("Wk", phi_k))):
                    if m == 0 and pi == 1:
                        K.wait("pe", wt_evt["Wk"])
                    for i in range(8):
                        r = (pi * 8 + i) % 4
                        bank = qk_guards[r]
                        nb = qk_banks[r]
                        bank.before_write("pe")
                        if i == 0 and pi == 0:
                            g_xnfm.before_read("pe")
                            g_phi.before_write("dve")
                        for kc in range(8):
                            K.pe(lambda e, i=i, kc=kc, wkey=wkey, nb=nb: e.matmul(
                                nb[:, :], wt[wkey][:, kc, 128 * i:128 * (i + 1)],
                                xn_fm[:, kc, :], start=(kc == 0), stop=(kc == 7)), sig=(kc == 7))
                        ev_q = ("pe", K.cnt["pe"])
                        bank.wrote(ev_q)
                        esc = esc_slots[r]
                        eg = esc_guards[r]
                        eg.before_write("act")
                        bank.before_read("act")
                        ev_e = K.act(lambda e, esc=esc, nb=nb: e.activation(esc, nb[:, :], AF.Exp))
                        eg.wrote(ev_e)
                        K.wait("dve", ev_e)
                        K.dve(lambda e, esc=esc: e.tensor_scalar(esc, esc, 1.0, None, ALU.min))
                        ev_phi = K.dve(lambda e, i=i, esc=esc, nb=nb, phi_dst=phi_dst: e.scalar_tensor_tensor(
                            phi_dst[:, i, :], nb[:, :], 0.0, esc, ALU.max, ALU.add))
                        bank.read(ev_e, ev_phi)
                        eg.read(ev_phi)
                g_phi.wrote(("dve", K.cnt["dve"]))

            sc_pos = {}
            score_sets = [(1, [0, 2, 4, 6]), (0, [8, 10, 12, 14]),
                          (1, [1, 3, 5, 7]), (0, [9, 11, 13, 15])]
            for _si, (_b, _hs) in enumerate(score_sets):
                for _j, _h in enumerate(_hs):
                    sc_pos[_h] = 4 * _si + _j
            set_of_head = {h: si for si, (_b, hs) in enumerate(score_sets) for h in hs}

            def emit_scores(c, set_i):
                p = c % 2
                cm = c % MC
                bno, heads = score_sets[set_i]
                bank = gS[bno]
                bank.before_write("pe")
                g_phi.before_read("pe")
                for j, h in enumerate(heads):
                    K.pe(lambda e, h=h, j=j, cm=cm, bno=bno: e.matmul(
                        bS[bno][:, 128 * j:128 * (j + 1)],
                        phi_k[64 * (h % 2):64 * (h % 2) + 64, h // 2, 128 * cm:128 * (cm + 1)],
                        phi_q[64 * (h % 2):64 * (h % 2) + 64, h // 2, 128 * cm:128 * (cm + 1)],
                        start=True, stop=True), sig=(j == 3))
                ev_sc = ("pe", K.cnt["pe"])
                bank.wrote(ev_sc)

            def emit_mask(c, set_i):
                p = c % 2
                bno, heads = score_sets[set_i]
                bank = gS[bno]
                g_scsb[p][set_i].before_write("dve")
                bank.before_read("dve")
                ev = K.dve(lambda e, bno=bno, set_i=set_i, p=p: e.tensor_tensor(
                    sc_sb[p][:, 512 * set_i:512 * (set_i + 1)].rearrange("p (j t) -> p j t", t=128),
                    bS[bno][:, :].rearrange("p (j t) -> p j t", t=128),
                    mask[:, :].rearrange("p (o t) -> p o t", o=1).broadcast_to((128, 4, 128)),
                    ALU.mult))
                bank.read(ev)
                g_scsb[p][set_i].wrote(ev)
                for h in heads:
                    chs[c]["mask"][h] = ev

            def emit_num_group(b, grp):
                st_b = chs[b]
                pb = b % 2
                cmb = b % MC
                bank = gN[grp % 2]
                nb = bN[grp % 2]
                bank.before_write("pe")
                K.wait_all("pe", [st_b["mask"][4 * grp + j] for j in range(4)])
                if grp == 0 and not st_b["first"]:
                    g_sbf[(b - 1) % 2].before_read("pe")
                if not st_b["first"]:
                    g_phi.before_read("pe")
                for j in range(4):
                    h = 4 * grp + j
                    K.pe(lambda e, h=h, j=j, pb=pb, nb=nb: e.matmul(
                        nb[:, 66 * j:66 * j + 65],
                        sc_sb[pb][:, 128 * sc_pos[h]:128 * (sc_pos[h] + 1)],
                        v_sb[pb][:, 66 * h:66 * h + 65],
                        start=(j == 0), stop=(st_b["first"] and j == 3)),
                        sig=(st_b["first"] and j == 3))
                if not st_b["first"]:
                    for j in range(4):
                        h = 4 * grp + j
                        K.pe(lambda e, h=h, j=j, cmb=cmb, b=b, nb=nb: e.matmul(
                            nb[:, 66 * j:66 * j + 65],
                            phi_q[:, h // 2, 128 * cmb:128 * (cmb + 1)],
                            s_bf[(b - 1) % 2][:, 66 * h:66 * h + 65],
                            start=False, stop=(j == 3)), sig=(j == 3))
                ev_g = ("pe", K.cnt["pe"])
                bank.wrote(ev_g)
                st_b["numg"][grp] = ev_g
                g_scsb[pb][set_of_head[4 * grp]].read(ev_g)
                g_scsb[pb][set_of_head[4 * grp + 1]].read(ev_g)
                g_vsb[pb].read(ev_g)
                if not st_b["first"]:
                    g_sbf[(b - 1) % 2].read(ev_g)
                    g_phi.read(ev_g)

            def emit_epi(b, grp):
                st_b = chs[b]
                pb = b % 2
                bank = gN[grp % 2]
                nb = bN[grp % 2]
                bank.before_read("dve")
                if grp == 0:
                    g_atok[pb].before_write("dve")
                K.dve(lambda e, nb=nb: e.tensor_scalar(
                    den_t[:, 0:4],
                    nb[:, 0:264].rearrange("p (j w) -> p j w", w=66)[:, :, 64],
                    EPS, None, ALU.add))
                K.dve(lambda e: e.reciprocal(rec_t[:, 0:4], den_t[:, 0:4]))
                ev_att = K.dve(lambda e, grp=grp, pb=pb, nb=nb: e.tensor_tensor(
                    att_tok[pb][:, 256 * grp:256 * (grp + 1)].rearrange("p (j w) -> p j w", w=64),
                    nb[:, 0:264].rearrange("p (j w) -> p j w", w=66)[:, :, 0:64],
                    rec_t[:, 0:4].rearrange("p (j o) -> p j o", o=1).broadcast_to((128, 4, 64)),
                    ALU.mult))
                bank.read(ev_att)
                st_b["epi"][grp] = ev_att
                if grp == 3:
                    g_atok[pb].wrote(*st_b["epi"])

            def emit_attT(b2):
                st2 = chs[b2]
                p2 = b2 % 2
                gT.before_write("pe")
                g_atok[p2].before_read("pe")
                K.wait_all("pe", st2["epi"])
                for ic in range(8):
                    K.pe(lambda e, ic=ic, p2=p2: e.transpose(
                        bT[:, 128 * ic:128 * (ic + 1)], att_tok[p2][:, 128 * ic:128 * (ic + 1)], ident[:, :]),
                        sig=(ic == 7))
                ev_at = ("pe", K.cnt["pe"])
                gT.wrote(ev_at)
                g_atok[p2].read(ev_at)
                g_afm.before_write("dve")
                gT.before_read("dve")
                ev_afm = K.dve(lambda e: e.tensor_copy(
                    att_fm[:, :, :], bT[:, :].rearrange("p (ic t) -> p ic t", t=128)))
                gT.read(ev_afm)
                g_afm.wrote(ev_afm)

            def emit_out2(b2):
                st2 = chs[b2]
                p2 = b2 % 2
                cmb2 = b2 % MC
                s_i = b2 // CPS
                row = (b2 % CPS) * CH
                for bk in (0, 1):
                    gP[bk].before_write("pe")
                g_afm.before_read("pe")
                if b2 == 0:
                    K.wait("pe", wt_evt["Wo"])
                    K.wait("pe", m_ready)
                    K.wait("pe", bias_ready)
                for ic in range(8):
                    for bk in (0, 1):
                        K.pe(lambda e, ic=ic, bk=bk: e.matmul(
                            bP[bk][:, :], att_fm[:, ic, :], wt["Wo"][:, ic, 512 * bk:512 * (bk + 1)],
                            start=(ic == 0), stop=False))
                for kc in range(8):
                    for bk in (0, 1):
                        K.pe(lambda e, kc=kc, bk=bk, cmb2=cmb2: e.matmul(
                            bP[bk][:, :], xn_fm[:, kc, 128 * cmb2:128 * (cmb2 + 1)],
                            m_sb[:, kc, 512 * bk:512 * (bk + 1)], start=False, stop=False))
                ev_out = [None, None]
                for bk in (0, 1):
                    K.pe(lambda e, bk=bk: e.matmul(
                        bP[bk][:, :], ones2[0:1, :], bias_hi[0:1, 512 * bk:512 * (bk + 1)],
                        start=False, stop=False))
                    ev_out[bk] = K.pe(lambda e, bk=bk: e.matmul(
                        bP[bk][:, :], ones2[0:1, :], bias_lo[0:1, 512 * bk:512 * (bk + 1)],
                        start=False, stop=True), sig=True)
                    gP[bk].wrote(ev_out[bk])
                g_xnfm.read(ev_out[1])
                g_afm.read(ev_out[1])
                st2["out"] = ev_out

            def emit_final2(b2):
                st2 = chs[b2]
                ob = out_sb[b2 % 2]
                g_obuf[b2 % 2].before_write("dve")
                ev_fin = [None, None]
                for bk in (0, 1):
                    K.wait("dve", st2["out"][bk])
                    gP[bk].before_read("dve")
                    ev_fin[bk] = K.dve(lambda e, bk=bk, b2=b2, ob=ob: e.tensor_tensor(
                        ob[:, 512 * bk:512 * (bk + 1)], bP[bk][:, :],
                        x_ch[b2 % 7][:, 512 * bk:512 * (bk + 1)], ALU.add))
                    gP[bk].read(ev_fin[bk])
                g_xbuf[b2 % 7].read(ev_fin[1])
                st2["final"] = ev_fin[1]

            def emit_store2(b2):
                st2 = chs[b2]
                s_i = b2 // CPS
                row = (b2 % CPS) * CH
                ob = out_sb[b2 % 2]
                K.wait("act", st2["final"])
                ev_st = K.dma("act", f"st{b2 % 2}",
                              lambda e, s_i=s_i, row=row, ob=ob: e.dma_start(
                                  y_d[s_i, row:row + CH, :], ob[:, :]))
                g_obuf[b2 % 2].wrote(ev_st)
                K.wait("sync", st2["final"])
                load_chunk(b2 + 7)

            def emit_A_v(c):
                p = c % 2
                cm = c % MC
                for bk in (0, 1):
                    gP[bk].before_write("pe")
                g_xnfm.before_read("pe")
                if c == 0:
                    K.wait("pe", wt_evt["Wv"])
                for kc in range(8):
                    for bk in (0, 1):
                        K.pe(lambda e, kc=kc, bk=bk, cm=cm: e.matmul(
                            bP[bk][:, :], xn_fm[:, kc, 128 * cm:128 * (cm + 1)],
                            wt["Wv"][:, kc, 512 * bk:512 * (bk + 1)],
                            start=(kc == 0), stop=(kc == 7)), sig=(kc == 7 and bk == 1))
                ev_v = ("pe", K.cnt["pe"])
                for bk in (0, 1):
                    gP[bk].wrote(ev_v)
                g_xnfm.read(ev_v)
                g_vsb[p].before_write("act")
                evs = []
                for bk in (0, 1):
                    gP[bk].before_read("act")
                    ev = K.act(lambda e, bk=bk, p=p: e.copy(
                        v_sb[p][:, :].rearrange("p (h w) -> p h w", w=66)[:, 8 * bk:8 * (bk + 1), 0:64],
                        bP[bk][:, :].rearrange("p (h w) -> p h w", w=64)))
                    gP[bk].read(ev)
                    evs.append(ev)
                g_vsb[p].wrote(*evs)

            def emit_A_pkT(c):
                p = c % 2
                cm = c % MC
                gT.before_write("pe")
                g_phi.before_read("pe")
                for ic in range(8):
                    K.pe(lambda e, ic=ic, cm=cm: e.transpose(
                        bT[:, 128 * ic:128 * (ic + 1)], phi_k[:, ic, 128 * cm:128 * (cm + 1)], ident[:, :]),
                        sig=(ic == 7))
                ev_pkt = ("pe", K.cnt["pe"])
                gT.wrote(ev_pkt)
                g_pktok[p].before_write("act")
                gT.before_read("act")
                ev_pk = K.act(lambda e, p=p: e.copy(
                    pk_tok[p][:, :].rearrange("p (ic t) -> p ic t", t=128),
                    bT[:, :].rearrange("p (ic t) -> p ic t", t=128)))
                gT.read(ev_pk)
                g_pktok[p].wrote(ev_pk)

            def emit_Z(c):
                p = c % 2
                gS[0].before_write("pe")
                g_pktok[p].before_read("pe")
                for c2 in range(8):
                    K.pe(lambda e, c2=c2, p=p: e.matmul(
                        bS[0][:, c2:c2 + 1], pk_tok[p][:, 128 * c2:128 * (c2 + 1)], onescol[:, 0:1],
                        start=(c2 == 0), stop=(c2 == 7)), sig=(c2 == 7))
                chs[c]["z"] = ("pe", K.cnt["pe"])
                gS[0].wrote(chs[c]["z"])

            def emit_Zadds(c):
                g_zb.before_write("dve")
                K.wait("dve", chs[c]["z"])
                if chs[c]["first"]:
                    ev = K.dve(lambda e: e.tensor_copy(zb[:, :], bS[0][:, 0:8]))
                else:
                    ev = K.dve(lambda e: e.tensor_tensor(zb[:, :], bS[0][:, 0:8], zb[:, :], ALU.add))
                gS[0].read(ev)
                g_zb.wrote(ev)
                chs[c]["zadd"] = ev

            def emit_SD(c, half):
                p = c % 2
                if half == 0:
                    gSD.before_write("pe")
                g_vsb[p].before_read("pe")
                lo = 64 * half
                first = chs[c]["first"]
                for c2 in range(8):
                    K.pe(lambda e, c2=c2, p=p, lo=lo, half=half, first=first: e.matmul(
                        bSD[lo:lo + 64, 64 * c2:64 * (c2 + 1)],
                        pk_tok[p][:, 128 * c2 + lo:128 * c2 + lo + 64],
                        v_sb[p][:, 66 * (2 * c2 + half):66 * (2 * c2 + half) + 64],
                        start=(c2 == 0 and first), stop=(c2 == 7 and half == 1),
                        skip_group_check=True), sig=(c2 == 7))
                ev_sd = ("pe", K.cnt["pe"])
                g_vsb[p].read(ev_sd)
                g_pktok[p].read(ev_sd)
                if half == 1:
                    gSD.wrote(ev_sd)
                    chs[c]["sd1"] = ev_sd

            def emit_shadow(c):
                p = c % 2
                g_sbf[p].before_write("act")
                gSD.before_read("act")
                K.wait("act", chs[c]["sd1"])
                K.wait("act", chs[c]["zadd"])
                evs = []
                for lo, off in ((0, 0), (64, 1)):
                    evs.append(K.act(lambda e, p=p, lo=lo, off=off: e.copy(
                        s_bf[p][lo:lo + 64, :].rearrange("p (h w) -> p h w", w=66)[:, off::2, 0:64],
                        bSD[lo:lo + 64, :].rearrange("p (q w) -> p q w", w=64))))
                    evs.append(K.act(lambda e, p=p, lo=lo, off=off: e.copy(
                        s_bf[p][lo:lo + 64, :].rearrange("p (h w) -> p h w", w=66)[:, off::2, 64],
                        zb[lo:lo + 64, :].rearrange("p (q o) -> p q o", o=1))))
                gSD.read(*evs)
                g_zb.read(evs[1], evs[3])
                g_sbf[p].wrote(evs[-1])

            NCH_ = NCH
            next_num = 0
            next_out = 0

            def emit_out_stage(b2):
                emit_attT(b2)
                emit_out2(b2)
                emit_final2(b2)
                emit_store2(b2)

            for c in range(NCH_):
                chs[c] = {"mask": {}, "numg": [None] * 4, "epi": [None] * 4,
                          "first": (c % CPS == 0)}
                if c % MC == 0:
                    m = c // MC
                    if c > 0:
                        b = c - 1
                        assert next_num == b and next_out == c - 2
                        ev_nmr = pre_nmr[m]
                        xe = [None] * MC
                        emit_attT(c - 2)
                        emit_num_group(b, 0)
                        emit_epi(b, 0)
                        xe[0] = emit_xnT(m, 0, ev_nmr)
                        emit_num_group(b, 1)
                        emit_epi(b, 1)
                        xe[1] = emit_xnT(m, 1, ev_nmr)
                        emit_out2(c - 2)
                        emit_final2(c - 2)
                        emit_store2(c - 2)
                        xe[2] = emit_xnT(m, 2, ev_nmr)
                        emit_num_group(b, 2)
                        emit_epi(b, 2)
                        emit_num_group(b, 3)
                        emit_epi(b, 3)
                        next_num = c
                        emit_attT(b)
                        emit_out2(b)
                        xe[3] = emit_xnT(m, 3, ev_nmr)
                        emit_final2(b)
                        emit_store2(b)
                        next_out = c
                    else:
                        g_stats[0].before_write("dve")
                        for cc in range(MC):
                            emit_stats_chunk(0, cc)
                        ev_nmr = emit_rstd(0)
                        for cc in range(MC):
                            emit_normalize(0, cc)
                        xe = [emit_xnT(m, cc, ev_nmr) for cc in range(MC)]
                    emit_qk(m, xe)
                    emit_A_pkT(c)
                    emit_A_v(c)
                    emit_scores(c, 0)
                    emit_mask(c, 0)
                    emit_scores(c, 1)
                    emit_mask(c, 1)
                    emit_scores(c, 2)
                    emit_mask(c, 2)
                    emit_Z(c)
                    emit_Zadds(c)
                    emit_scores(c, 3)
                    emit_mask(c, 3)
                    emit_SD(c, 0)
                    emit_SD(c, 1)
                    emit_shadow(c)
                else:
                    b = next_num
                    b2 = next_out if next_out == c - 2 else None
                    emit_scores(c, 0)
                    emit_scores(c, 1)
                    if b2 is not None:
                        emit_attT(b2)
                    emit_mask(c, 0)
                    emit_mask(c, 1)
                    emit_A_v(c)
                    if b2 is not None:
                        emit_out2(b2)
                    emit_A_pkT(c)
                    emit_num_group(b, 0)
                    emit_epi(b, 0)
                    emit_num_group(b, 1)
                    emit_Z(c)
                    emit_Zadds(c)
                    if b2 is not None:
                        emit_final2(b2)
                        emit_store2(b2)
                        next_out = b2 + 1
                    emit_epi(b, 1)
                    emit_scores(c, 2)
                    emit_mask(c, 2)
                    emit_scores(c, 3)
                    emit_mask(c, 3)
                    emit_num_group(b, 2)
                    emit_epi(b, 2)
                    emit_num_group(b, 3)
                    emit_epi(b, 3)
                    emit_SD(c, 0)
                    emit_SD(c, 1)
                    emit_shadow(c)
                    next_num = b + 1
                mm = c // MC + 1
                if mm * MC < NCH_:
                    if c % MC == 2:
                        g_stats[mm % 2].before_write("dve")
                        emit_stats_chunk(mm, 0)
                        emit_stats_chunk(mm, 1)
                        emit_stats_chunk(mm, 2)
                    elif c % MC == 3:
                        emit_stats_chunk(mm, 3)
                        emit_rstd(mm)
                        for cc in range(MC):
                            emit_normalize(mm, cc)
                if c % (4 * MC) == 4 * MC - 1:
                    K.flush()
            while next_num < NCH_:
                for grp in range(4):
                    emit_num_group(next_num, grp)
                    emit_epi(next_num, grp)
                next_num += 1
            while next_out < NCH_:
                emit_out_stage(next_out)
                next_out += 1
            K.flush()

            K.wait("sync", ("st0", K.dmac["st0"]))
            K.wait("sync", ("st1", K.dmac["st1"]))
            K.flush()

    return nc


def kernel(**inputs):
    x = np.asarray(inputs["L0_attn_x"])
    n_seq_total, L, _ = x.shape
    per = n_seq_total // N_CORES
    nc = build(per, L // MAC)
    consts = _consts()
    keys = ["L0_attn_Wq", "L0_attn_Wk", "L0_attn_Wv", "L0_attn_Wo", "L0_attn_bo",
            "L0_ffn_G1_R", "L0_ffn_G2_R", "L0_ffn_G1_I", "L0_ffn_G2_I",
            "L0_ffn_bias_real", "L0_ffn_bias_imag"]
    in_maps = []
    for core in range(N_CORES):
        m = {"L0_attn_x": np.ascontiguousarray(x[core * per:(core + 1) * per])}
        for k in keys:
            m[k] = np.asarray(inputs[k])
        m.update(consts)
        in_maps.append(m)
    res = run_bass_kernel_spmd(nc, in_maps, core_ids=list(range(N_CORES)))
    out = np.concatenate([np.asarray(r["y"]) for r in res.results], axis=0)
    return out.astype(np.float32, copy=False)


# version 25
# speedup vs baseline: 1.0193x; 1.0193x over previous
import numpy as np
import ml_dtypes

import concourse.bass as bass
import concourse.mybir as mybir
from concourse.bass_utils import run_bass_kernel_spmd

F32 = mybir.dt.float32
BF16 = mybir.dt.bfloat16
AF = mybir.ActivationFunctionType
ALU = mybir.AluOpType
AX = mybir.AxisListType
BF16NP = ml_dtypes.bfloat16

H = 16
DH = 64
E = 1024
CH = 128
MC = 4
MAC = CH * MC
EPS = 1e-5
N_CORES = 8
STAGE = 99
DBG = None


def _consts():
    c = {}
    c["c_ident"] = np.eye(128, dtype=BF16NP)
    s = np.arange(128)[:, None]
    t = np.arange(128)[None, :]
    c["c_mask"] = (s <= t).astype(BF16NP)
    rcolA = np.zeros((32, 512), np.float32)
    for b in range(32):
        rcolA[b, b * 16:(b + 1) * 16] = 1.0
    c["c_rcolA"] = rcolA.astype(BF16NP)
    rrA = np.zeros((32, 4, 128), np.float32)
    for p in range(4):
        for dg in range(128):
            rrA[8 * p + dg // 16, p, dg] = 1.0
    c["c_rrowA"] = rrA.reshape(32, 512).astype(BF16NP)
    rcolB = np.zeros((16, 512), np.float32)
    for b in range(32):
        for e in range(16):
            rcolB[e, b * 16 + e] = 1.0
    c["c_rcolB"] = rcolB.astype(BF16NP)
    rrB = np.zeros((16, 128), np.float32)
    for dg in range(128):
        rrB[dg % 16, dg] = 1.0
    c["c_rrowB"] = rrB.astype(BF16NP)
    c["c_ones2"] = np.ones((2, 128), BF16NP)
    c["c_onescol"] = np.ones((128, 1), BF16NP)
    return c


class Ctx:

    ENG = {"pe": "tensor", "dve": "vector", "act": "scalar", "sync": "sync", "gp": "gpsimd"}

    def __init__(self, nc, block, sems):
        self.nc = nc
        self.block = block
        self.sem = sems
        self.cnt = {"pe": 0, "dve": 0, "act": 0}
        self.dmac = {"wg": 0, **{f"w{i}": 0 for i in range(16)},
                     "ld0": 0, "ld1": 0, "ld2": 0, "ld3": 0, "ld4": 0, "ld5": 0, "ld6": 0,
                     "st0": 0, "st1": 0}
        self.stage = {k: [] for k in self.ENG}

    def wait(self, eng, evt):
        if evt is None:
            return
        sname, val = evt
        if val <= 0:
            return
        if sname in self.cnt:
            assert val <= self.cnt[sname], f"wait on future evt {sname}={val} (cur {self.cnt[sname]})"
        else:
            assert val <= self.dmac[sname], f"wait on future dma {sname}={val}"
        sem = self.sem[sname]
        self.stage[eng].append(lambda e, sem=sem, val=val: e.wait_ge(sem, val))

    def wait_all(self, eng, evts):
        merged = {}
        for evt in evts:
            if evt is None:
                continue
            s, v = evt
            merged[s] = max(merged.get(s, 0), v)
        for s, v in merged.items():
            self.wait(eng, (s, v))

    def pe(self, f, sig=False):
        if sig:
            self.cnt["pe"] += 1
            n = self.cnt["pe"]
            sem = self.sem["pe"]
            self.stage["pe"].append(lambda e, f=f: f(e).then_inc(sem, 1))
            return ("pe", n)
        self.stage["pe"].append(lambda e, f=f: f(e))
        return None

    def _chained(self, eng, f):
        n0 = self.cnt[eng]
        self.cnt[eng] = n0 + 1
        sem = self.sem[eng]

        def run(e, f=f, n0=n0, sem=sem):
            inst = f(e)
            if n0 > 0:
                inst._wait_ge(sem, n0)
            inst.then_inc(sem, 1)

        self.stage[eng].append(run)
        return (eng, n0 + 1)

    def dve(self, f):
        return self._chained("dve", f)

    def act(self, f):
        return self._chained("act", f)

    def dma(self, eng, ring, f):
        self.dmac[ring] += 16
        v = self.dmac[ring]
        sem = self.sem[ring]
        self.stage[eng].append(lambda e, f=f, sem=sem: f(e).then_inc(sem, 16))
        return (ring, v)

    def flush(self):
        for eng, lst in self.stage.items():
            if not lst:
                continue

            def runner(e, lst=tuple(lst)):
                for t in lst:
                    t(e)

            getattr(self.block, self.ENG[eng])(runner)
        self.stage = {k: [] for k in self.ENG}


class Guard:

    def __init__(self, ctx):
        self.ctx = ctx
        self.w = []
        self.r = []

    def before_write(self, eng):
        self.ctx.wait_all(eng, self.r if self.r else self.w)

    def wrote(self, *evts):
        self.w = [e for e in evts if e is not None]
        self.r = []

    def before_read(self, eng):
        self.ctx.wait_all(eng, self.w)

    def read(self, *evts):
        self.r += [e for e in evts if e is not None]


def build(n_seq, n_mac):
    nc = bass.Bass("TRN2", target_bir_lowering=False)
    L = n_mac * MAC

    x_d = nc.dram_tensor("L0_attn_x", [n_seq, L, E], F32, kind="ExternalInput").ap()
    y_d = nc.dram_tensor("y", [n_seq, L, E], F32, kind="ExternalOutput").ap()
    w_d = {k: nc.dram_tensor(f"L0_attn_{k}", [E, E], F32, kind="ExternalInput").ap()
           for k in ("Wq", "Wk", "Wv", "Wo")}
    bo_d = nc.dram_tensor("L0_attn_bo", [E], F32, kind="ExternalInput").ap()
    g_d = {k: nc.dram_tensor(f"L0_ffn_{k}", [1, 32, 4, 32] if k.startswith("G1") else [4, 16, 1, 16],
                             F32, kind="ExternalInput").ap()
           for k in ("G1_R", "G2_R", "G1_I", "G2_I")}
    br_d = nc.dram_tensor("L0_ffn_bias_real", [512], F32, kind="ExternalInput").ap()
    bi_d = nc.dram_tensor("L0_ffn_bias_imag", [512], F32, kind="ExternalInput").ap()
    cd = {}
    for k, arr in _consts().items():
        cd[k] = nc.dram_tensor(k, list(arr.shape), BF16 if arr.dtype == BF16NP else F32,
                               kind="ExternalInput").ap()

    import contextlib
    ctxmgr = contextlib.ExitStack()
    sb = lambda name, shape, dt: ctxmgr.enter_context(nc.sbuf_tensor(name, shape, dt))
    ps = lambda name, shape, dt: ctxmgr.enter_context(nc.psum_tensor(name, shape, dt))

    with ctxmgr:
        wt = {k: sb(f"wt_{k}", [128, 8, 1024], BF16) for k in ("Wq", "Wk", "Wv", "Wo")}
        x_ch = [sb(f"x{i}", [128, 1024], F32) for i in range(7)]
        xn_tok = sb("xn_tok", [128, 1024], BF16)
        xn_pre = sb("xn_pre", [128, 4, 1024], BF16)
        xn_fm = sb("xn_fm", [128, 8, MAC], BF16)
        phi_q = sb("phi_q", [128, 8, MAC], BF16)
        phi_k = sb("phi_k", [128, 8, MAC], BF16)
        pk_tok = [sb(f"pk_tok{i}", [128, 1024], BF16) for i in range(2)]
        v_sb = [sb(f"v_sb{i}", [128, 16 * 66], BF16) for i in range(2)]
        sc_sb = [sb(f"sc_sb{i}", [128, 2048], BF16) for i in range(2)]
        att_tok = [sb(f"att_tok{i}", [128, 1024], BF16) for i in range(2)]
        att_fm = sb("att_fm", [128, 8, 128], BF16)
        out_sb = [sb(f"out{i}", [128, 1024], F32) for i in range(2)]
        zb = sb("zb", [128, 8], F32)
        s_bf = [sb(f"s_bf{i}", [128, 16 * 66], BF16) for i in range(2)]
        den_t = sb("den_t", [128, 4], F32)
        rec_t = sb("rec_t", [128, 4], F32)
        st6 = [sb(f"st6_{i}", [128, 12], F32) for i in range(2)]
        mv4 = [sb(f"mv4_{i}", [128, 4, 2], F32) for i in range(2)]
        lnv4 = sb("lnv4", [128, 4], F32)
        rstd4 = [sb(f"rstd4_{i}", [128, 4], F32) for i in range(2)]
        nmr4 = [sb(f"nmr4_{i}", [128, 4], F32) for i in range(2)]
        epsc = sb("epsc", [128, 1], F32)
        ident = sb("ident", [128, 128], BF16)
        mask = sb("mask", [128, 128], BF16)
        rcA = sb("rcA", [32, 512], BF16)
        rrA = sb("rrA", [32, 512], BF16)
        rcB = sb("rcB", [16, 512], BF16)
        rrB = sb("rrB", [16, 128], BF16)
        ones2 = sb("ones2", [2, 128], BF16)
        onescol = sb("onescol", [128, 1], BF16)
        wstage = [sb(f"wstg{i}", [128, 8, 1024], BF16) for i in range(2)]
        m_sb = wstage[0]
        g1_sb = {p: sb(f"g1{p}", [32, 128], BF16) for p in "RI"}
        g2_sb = {p: sb(f"g2{p}", [16, 64], BF16) for p in "RI"}

        bias_hi = sb("bias_hi", [1, 1024], BF16)
        bias_lo = sb("bias_lo", [1, 1024], BF16)
        bP = [ps("bP0", [128, 512], F32), ps("bP1", [128, 512], F32)]
        bS = [ps("bS0", [128, 512], F32), ps("bS1", [128, 512], F32)]
        bN = [ps("bN0", [128, 512], F32), ps("bN1", [128, 512], F32)]
        bT = ps("bT", [128, 1024], BF16)
        bSD = ps("bSD", [128, 512], F32)

        import contextlib as _ctl
        semstack = ctxmgr
        semnames = ["pe", "dve", "act", "wg"] + [f"w{i}" for i in range(16)] + [
                    "ld0", "ld1", "ld2", "ld3", "ld4", "ld5", "ld6", "st0", "st1"]
        sems = {n: ctxmgr.enter_context(nc.semaphore(f"s_{n}")) for n in semnames}
        with nc.Block() as block:
            K = Ctx(nc, block, sems)
            gP = [Guard(K), Guard(K)]
            gS = [Guard(K), Guard(K)]
            gN = [Guard(K), Guard(K)]
            gT = Guard(K)
            g_xbuf = [Guard(K) for _ in range(7)]
            g_obuf = [Guard(K), Guard(K)]

            g_stats = [Guard(K), Guard(K)]
            g_xntok = Guard(K)
            g_xnpre = Guard(K)
            g_xnfm = Guard(K)
            g_phi = Guard(K)
            g_pktok = [Guard(K), Guard(K)]
            g_vsb = [Guard(K), Guard(K)]
            g_scsb = [[Guard(K) for _ in range(4)] for _ in range(2)]
            g_atok = [Guard(K), Guard(K)]
            g_afm = Guard(K)
            gSD = Guard(K)
            g_sbf = [Guard(K), Guard(K)]
            g_zb = Guard(K)
            bo_sb = out_sb[0]
            bcat = out_sb[1]
            tmp_f = out_sb[0]
            def accR_ap(p0):
                return x_ch[3 + p0 // 2][:, 512 * (p0 % 2):512 * (p0 % 2) + 512]

            def accI_ap(p0):
                return x_ch[5 + p0 // 2][:, 512 * (p0 % 2):512 * (p0 % 2) + 512]

            for dst, src in ((ident, cd["c_ident"]), (mask, cd["c_mask"]),
                             (rcA, cd["c_rcolA"]), (rrA, cd["c_rrowA"]),
                             (rcB, cd["c_rcolB"]), (rrB, cd["c_rrowB"]),
                             (ones2, cd["c_ones2"]), (onescol, cd["c_onescol"])):
                K.dma("gp", "wg", lambda e, d=dst, s=src: e.dma_start(d[:, :], s))
            K.dma("gp", "wg", lambda e: e.dma_start(bo_sb[0:1, 0:1024], bo_d.rearrange("(o k) -> o k", o=1)))
            K.dma("gp", "wg", lambda e: e.dma_start(bcat[0:1, 0:512], br_d.rearrange("(o k) -> o k", o=1)))
            K.dma("gp", "wg", lambda e: e.dma_start(bcat[0:1, 512:1024], bi_d.rearrange("(o k) -> o k", o=1)))
            for p in "RI":
                K.dma("gp", "wg",
                      lambda e, p=p: e.dma_start(g1_sb[p][:, :], g_d[f"G1_{p}"].rearrange("o b c d -> (o b) (c d)")))
                K.dma("gp", "wg",
                      lambda e, p=p: e.dma_start(g2_sb[p][:, :].rearrange("p (c g) -> p c g", g=16),
                                                 g_d[f"G2_{p}"].rearrange("c e o g -> (e o) c g")))
            small_done = ("wg", K.dmac["wg"])
            w_evt = {}

            def load_weight(k, idx, war_evt=None):
                K.wait("gp", war_evt)
                evs = []
                for q in range(4):
                    ring = f"w{4 * idx + q}"
                    K.dma("gp", ring,
                          lambda e, k=k, idx=idx, q=q: e.dma_start(
                              wstage[idx % 2][:, 2 * q:2 * q + 2, :],
                              w_d[k].rearrange("(r p) k -> p r k", p=128)[:, 2 * q:2 * q + 2, :]))
                    evs.append((ring, K.dmac[ring]))
                w_evt[k] = evs

            load_weight("Wq", 0)
            load_weight("Wk", 1)

            NCH = n_seq * n_mac * MC
            x_evt = {}

            def load_chunk(j):
                if j >= NCH:
                    return
                s_i = j // (n_mac * MC)
                row = (j % (n_mac * MC)) * CH
                ring = f"ld{j % 7}"
                g_xbuf[j % 7].before_write("sync")
                K.dma("sync", ring,
                      lambda e, s_i=s_i, row=row: e.dma_start(
                          x_ch[j % 7][:, :], x_d[s_i, row:row + CH, :]))
                x_evt[j] = (ring, K.dmac[ring])
                g_xbuf[j % 7].wrote(x_evt[j])

            K.wait("dve", small_done)
            for _vb in v_sb:
                K.dve(lambda e, _vb=_vb: e.memset(_vb[:, :].rearrange("p (h w) -> p h w", w=66)[:, :, 64:65], 1.0))
                K.dve(lambda e, _vb=_vb: e.memset(_vb[:, :].rearrange("p (h w) -> p h w", w=66)[:, :, 65:66], 0.0))
            K.dve(lambda e: e.memset(zb[:, :], 0.0))
            for _sbf in s_bf:
                K.dve(lambda e, _sbf=_sbf: e.memset(_sbf[:, :], 0.0))
            K.dve(lambda e: e.memset(epsc[:, :], EPS))
            evb1 = K.dve(lambda e: e.tensor_add(bcat[0:1, 0:512], bcat[0:1, 0:512], bo_sb[0:1, 0:512]))
            evb2 = K.dve(lambda e: e.tensor_add(bcat[0:1, 512:1024], bcat[0:1, 512:1024], bo_sb[0:1, 512:1024]))
            evb3 = K.dve(lambda e: e.tensor_copy(bias_hi[0:1, :], bcat[0:1, 0:1024]))
            evb4 = K.dve(lambda e: e.tensor_sub(bias_lo[0:1, :], bcat[0:1, 0:1024], bias_hi[0:1, :]))
            bias_ready = evb4


            wt_evt = {}

            wprep_pe = {}

            def weight_prep(k, idx):
                stg = wstage[idx % 2]
                K.wait_all("pe", w_evt[k])
                last_evac = None
                for r in range(8):
                    gT.before_write("pe")
                    for kk in range(8):
                        K.pe(lambda e, r=r, kk=kk, stg=stg: e.transpose(
                            bT[:, 128 * kk:128 * (kk + 1)], stg[:, r, 128 * kk:128 * (kk + 1)], ident[:, :]),
                            sig=(kk == 7))
                    evt_t = ("pe", K.cnt["pe"])
                    gT.wrote(evt_t)
                    K.wait("act", evt_t)
                    ev = K.act(lambda e, r=r, k=k: e.copy(
                        wt[k][:, :, 128 * r:128 * (r + 1)],
                        bT[:, :].rearrange("p (kk t) -> p kk t", t=128)))
                    gT.read(ev)
                    last_evac = ev
                wprep_pe[k] = ("pe", K.cnt["pe"])
                wt_evt[k] = last_evac

            K.wait("pe", small_done)
            for j in range(3):
                load_chunk(j)
            weight_prep("Wq", 0)
            load_weight("Wv", 2, war_evt=wprep_pe["Wq"])
            weight_prep("Wk", 1)
            load_weight("Wo", 3, war_evt=wprep_pe["Wk"])
            weight_prep("Wv", 2)
            weight_prep("Wo", 3)

            K.wait("pe", small_done)
            acc = {"R": accR_ap, "I": accI_ap}
            accev = {}
            g_t1s, g_t2s = Guard(K), Guard(K)
            t1s = xn_tok[0:32, 0:512]
            t2s = xn_tok[0:16, 512:1024]
            g_aS = g_afm
            aS = att_fm[:, :, :].rearrange("p a b -> p (a b)")[:, 0:512]
            for p in "RI":
                for c in range(4):
                    gS[0].before_write("pe")
                    e1 = K.pe(lambda e, p=p, c=c: e.matmul(
                        bS[0][0:32, :], g1_sb[p][0:32, 32 * c:32 * (c + 1)], rcA[0:32, :],
                        start=True, stop=True), sig=True)
                    gS[0].wrote(e1)
                    gS[0].before_read("act")
                    g_t1s.before_write("act")
                    ev1 = K.act(lambda e: e.copy(t1s[0:32, :], bS[0][0:32, :]))
                    g_t1s.wrote(ev1)
                    gS[0].read(ev1)
                    gS[1].before_write("pe")
                    e2 = K.pe(lambda e, p=p, c=c: e.matmul(
                        bS[1][0:16, :], g2_sb[p][0:16, 16 * c:16 * (c + 1)], rcB[0:16, :],
                        start=True, stop=True), sig=True)
                    gS[1].wrote(e2)
                    gS[1].before_read("act")
                    g_t2s.before_write("act")
                    ev2 = K.act(lambda e: e.copy(t2s[0:16, :], bS[1][0:16, :]))
                    g_t2s.wrote(ev2)
                    gS[1].read(ev2)
                    for p0 in range(4):
                        gN[0].before_write("pe")
                        K.wait("pe", ev1)
                        eA = K.pe(lambda e, p0=p0: e.matmul(
                            bN[0][:, :], rrA[0:32, 128 * p0:128 * (p0 + 1)], t1s[0:32, :],
                            start=True, stop=True), sig=True)
                        gN[0].wrote(eA)
                        gSD.before_write("pe")
                        K.wait("pe", ev2)
                        eB = K.pe(lambda e: e.matmul(
                            bSD[:, :], rrB[0:16, :], t2s[0:16, :],
                            start=True, stop=True), sig=True)
                        gSD.wrote(eB)
                        gN[0].before_read("act")
                        g_aS.before_write("act")
                        evA = K.act(lambda e: e.copy(aS[:, :], bN[0][:, :]))
                        g_aS.wrote(evA)
                        gN[0].read(evA)
                        gSD.before_read("dve")
                        K.wait("dve", evA)
                        if c == 0:
                            evP = K.dve(lambda e, p=p, p0=p0: e.tensor_mul(
                                acc[p](p0), bSD[:, :], aS[:, :]))
                        else:
                            K.dve(lambda e: e.tensor_mul(tmp_f[:, 0:512], bSD[:, :], aS[:, :]))
                            evP = K.dve(lambda e, p=p, p0=p0: e.tensor_add(
                                acc[p](p0), acc[p](p0), tmp_f[:, 0:512]))
                        gSD.read(evP)
                        g_aS.read(evP)
                        g_t1s.read(eA)
                        g_t2s.read(eB)
                accev[p] = evP
            m_parts = []
            K.wait("dve", wprep_pe["Wv"])
            for p0 in range(4):
                m_parts.append(K.dve(lambda e, p0=p0: e.tensor_copy(m_sb[:, p0, 0:512], accR_ap(p0))))
                m_parts.append(K.dve(lambda e, p0=p0: e.tensor_copy(m_sb[:, p0, 512:1024], accI_ap(p0))))
                m_parts.append(K.dve(lambda e, p0=p0: e.tensor_scalar(
                    m_sb[:, 4 + p0, 0:512], accI_ap(p0), -1.0, None, ALU.mult)))
                m_parts.append(K.dve(lambda e, p0=p0: e.tensor_copy(m_sb[:, 4 + p0, 512:1024], accR_ap(p0))))
            m_ready = m_parts[-1]

            K.wait("sync", m_ready)
            for j in range(3, 7):
                load_chunk(j)
            K.flush()

            chs = {}
            CPS = n_mac * MC

            pre_nmr = {}

            def emit_stats_chunk(mm, cc):
                s = mm % 2
                j = mm * MC + cc
                K.wait("dve", x_evt[j])
                K.dve(lambda e, j=j, s=s: e.bn_stats(st6[s][:, 0:6], x_ch[j % 7][:, 0:512]))
                K.dve(lambda e, j=j, s=s: e.bn_stats(st6[s][:, 6:12], x_ch[j % 7][:, 512:1024]))
                ev = K.dve(lambda e, cc=cc, s=s: e.bn_aggr(mv4[s][:, cc, 0:2], st6[s][:, :]))
                g_xbuf[j % 7].read(ev)

            def emit_rstd(mm):
                s = mm % 2
                ev_mv = ("dve", K.cnt["dve"])
                K.wait("act", ev_mv)
                K.act(lambda e, s=s: e.activation(lnv4[:, 0:4], mv4[s][:, :, 1], AF.Ln,
                                                  bias=epsc[:, 0:1], scale=1.0))
                ev_rstd = K.act(lambda e, s=s: e.activation(rstd4[s][:, 0:4], lnv4[:, 0:4], AF.Exp,
                                                            bias=0.0, scale=-0.5))
                K.wait("dve", ev_rstd)
                ev_nmr = K.dve(lambda e, s=s: e.scalar_tensor_tensor(
                    nmr4[s][:, 0:4], mv4[s][:, :, 0], -1.0, rstd4[s][:, 0:4], ALU.mult, ALU.mult))
                g_stats[s].wrote(ev_nmr, ev_rstd)
                pre_nmr[mm] = ev_nmr
                return ev_nmr

            nrm_ev = {}

            def emit_normalize(m, cc):
                s = m % 2
                j = m * MC + cc
                if cc == 0:
                    g_xnpre.before_write("act")
                    K.wait("act", pre_nmr[m])
                ev_xn = K.act(lambda e, j=j, cc=cc, s=s: e.activation(
                    xn_pre[:, cc, :], x_ch[j % 7][:, :], AF.Identity,
                    bias=nmr4[s][:, cc:cc + 1], scale=rstd4[s][:, cc:cc + 1]))
                g_stats[s].read(ev_xn)
                g_xbuf[j % 7].read(ev_xn)
                nrm_ev[(m, cc)] = ev_xn
                if cc == MC - 1:
                    g_xnpre.wrote(*[nrm_ev[(m, k)] for k in range(MC)])

            def emit_xnT(m, cc, ev_nmr):
                gT.before_write("pe")
                K.wait("pe", nrm_ev[(m, cc)])
                for ic in range(8):
                    K.pe(lambda e, ic=ic, cc=cc: e.transpose(
                        bT[:, 128 * ic:128 * (ic + 1)], xn_pre[:, cc, 128 * ic:128 * (ic + 1)], ident[:, :]),
                        sig=(ic == 7))
                ev_t = ("pe", K.cnt["pe"])
                gT.wrote(ev_t)
                g_xnpre.read(ev_t)
                gT.before_read("dve")
                if cc == 0:
                    g_xnfm.before_write("dve")
                ev_evac = K.dve(lambda e, cc=cc: e.tensor_copy(
                    xn_fm[:, :, 128 * cc:128 * (cc + 1)],
                    bT[:, :].rearrange("p (kk t) -> p kk t", t=128)))
                gT.read(ev_evac)
                return ev_evac

            def emit_qk(m, xn_evts):
                g_xnfm.wrote(*xn_evts)
                if m == 0:
                    K.wait("pe", wt_evt["Wq"])
                qk_banks = [bP[0], bP[1], bN[0], bN[1]]
                qk_guards = [gP[0], gP[1], gN[0], gN[1]]
                esc_slots = [xn_tok[:, 0:512], xn_tok[:, 512:1024],
                             att_tok[0][:, 0:512], att_tok[0][:, 512:1024]]
                esc_guards = [g_xntok, g_xntok, g_atok[0], g_atok[0]]
                for pi, (wkey, phi_dst) in enumerate((("Wq", phi_q), ("Wk", phi_k))):
                    if m == 0 and pi == 1:
                        K.wait("pe", wt_evt["Wk"])
                    for i in range(8):
                        r = (pi * 8 + i) % 4
                        bank = qk_guards[r]
                        nb = qk_banks[r]
                        bank.before_write("pe")
                        if i == 0 and pi == 0:
                            g_xnfm.before_read("pe")
                            g_phi.before_write("dve")
                        for kc in range(8):
                            K.pe(lambda e, i=i, kc=kc, wkey=wkey, nb=nb: e.matmul(
                                nb[:, :], wt[wkey][:, kc, 128 * i:128 * (i + 1)],
                                xn_fm[:, kc, :], start=(kc == 0), stop=(kc == 7)), sig=(kc == 7))
                        ev_q = ("pe", K.cnt["pe"])
                        bank.wrote(ev_q)
                        esc = esc_slots[r]
                        eg = esc_guards[r]
                        eg.before_write("act")
                        bank.before_read("act")
                        ev_e = K.act(lambda e, esc=esc, nb=nb: e.activation(esc, nb[:, :], AF.Exp))
                        eg.wrote(ev_e)
                        K.wait("dve", ev_e)
                        K.dve(lambda e, esc=esc: e.tensor_scalar(esc, esc, 1.0, None, ALU.min))
                        ev_phi = K.dve(lambda e, i=i, esc=esc, nb=nb, phi_dst=phi_dst: e.scalar_tensor_tensor(
                            phi_dst[:, i, :], nb[:, :], 0.0, esc, ALU.max, ALU.add))
                        bank.read(ev_e, ev_phi)
                        eg.read(ev_phi)
                g_phi.wrote(("dve", K.cnt["dve"]))

            sc_pos = {}
            score_sets = [(1, [0, 2, 4, 6]), (0, [8, 10, 12, 14]),
                          (1, [1, 3, 5, 7]), (0, [9, 11, 13, 15])]
            for _si, (_b, _hs) in enumerate(score_sets):
                for _j, _h in enumerate(_hs):
                    sc_pos[_h] = 4 * _si + _j
            set_of_head = {h: si for si, (_b, hs) in enumerate(score_sets) for h in hs}

            def emit_scores(c, set_i):
                p = c % 2
                cm = c % MC
                bno, heads = score_sets[set_i]
                bank = gS[bno]
                bank.before_write("pe")
                g_phi.before_read("pe")
                for j, h in enumerate(heads):
                    K.pe(lambda e, h=h, j=j, cm=cm, bno=bno: e.matmul(
                        bS[bno][:, 128 * j:128 * (j + 1)],
                        phi_k[64 * (h % 2):64 * (h % 2) + 64, h // 2, 128 * cm:128 * (cm + 1)],
                        phi_q[64 * (h % 2):64 * (h % 2) + 64, h // 2, 128 * cm:128 * (cm + 1)],
                        start=True, stop=True), sig=(j == 3))
                ev_sc = ("pe", K.cnt["pe"])
                bank.wrote(ev_sc)

            def emit_mask(c, set_i):
                p = c % 2
                bno, heads = score_sets[set_i]
                bank = gS[bno]
                g_scsb[p][set_i].before_write("dve")
                bank.before_read("dve")
                ev = K.dve(lambda e, bno=bno, set_i=set_i, p=p: e.tensor_tensor(
                    sc_sb[p][:, 512 * set_i:512 * (set_i + 1)].rearrange("p (j t) -> p j t", t=128),
                    bS[bno][:, :].rearrange("p (j t) -> p j t", t=128),
                    mask[:, :].rearrange("p (o t) -> p o t", o=1).broadcast_to((128, 4, 128)),
                    ALU.mult))
                bank.read(ev)
                g_scsb[p][set_i].wrote(ev)
                for h in heads:
                    chs[c]["mask"][h] = ev

            def emit_num_group(b, grp):
                st_b = chs[b]
                pb = b % 2
                cmb = b % MC
                bank = gN[grp % 2]
                nb = bN[grp % 2]
                bank.before_write("pe")
                K.wait_all("pe", [st_b["mask"][4 * grp + j] for j in range(4)])
                if grp == 0 and not st_b["first"]:
                    g_sbf[(b - 1) % 2].before_read("pe")
                if not st_b["first"]:
                    g_phi.before_read("pe")
                for j in range(4):
                    h = 4 * grp + j
                    K.pe(lambda e, h=h, j=j, pb=pb, nb=nb: e.matmul(
                        nb[:, 66 * j:66 * j + 65],
                        sc_sb[pb][:, 128 * sc_pos[h]:128 * (sc_pos[h] + 1)],
                        v_sb[pb][:, 66 * h:66 * h + 65],
                        start=(j == 0), stop=(st_b["first"] and j == 3)),
                        sig=(st_b["first"] and j == 3))
                if not st_b["first"]:
                    for j in range(4):
                        h = 4 * grp + j
                        K.pe(lambda e, h=h, j=j, cmb=cmb, b=b, nb=nb: e.matmul(
                            nb[:, 66 * j:66 * j + 65],
                            phi_q[:, h // 2, 128 * cmb:128 * (cmb + 1)],
                            s_bf[(b - 1) % 2][:, 66 * h:66 * h + 65],
                            start=False, stop=(j == 3)), sig=(j == 3))
                ev_g = ("pe", K.cnt["pe"])
                bank.wrote(ev_g)
                st_b["numg"][grp] = ev_g
                g_scsb[pb][set_of_head[4 * grp]].read(ev_g)
                g_scsb[pb][set_of_head[4 * grp + 1]].read(ev_g)
                g_vsb[pb].read(ev_g)
                if not st_b["first"]:
                    g_sbf[(b - 1) % 2].read(ev_g)
                    g_phi.read(ev_g)

            def emit_epi(b, grp):
                st_b = chs[b]
                pb = b % 2
                bank = gN[grp % 2]
                nb = bN[grp % 2]
                bank.before_read("dve")
                if grp == 0:
                    g_atok[pb].before_write("dve")
                K.dve(lambda e, nb=nb: e.tensor_scalar(
                    den_t[:, 0:4],
                    nb[:, 0:264].rearrange("p (j w) -> p j w", w=66)[:, :, 64],
                    EPS, None, ALU.add))
                K.dve(lambda e: e.reciprocal(rec_t[:, 0:4], den_t[:, 0:4]))
                ev_att = K.dve(lambda e, grp=grp, pb=pb, nb=nb: e.tensor_tensor(
                    att_tok[pb][:, 256 * grp:256 * (grp + 1)].rearrange("p (j w) -> p j w", w=64),
                    nb[:, 0:264].rearrange("p (j w) -> p j w", w=66)[:, :, 0:64],
                    rec_t[:, 0:4].rearrange("p (j o) -> p j o", o=1).broadcast_to((128, 4, 64)),
                    ALU.mult))
                bank.read(ev_att)
                st_b["epi"][grp] = ev_att
                if grp == 3:
                    g_atok[pb].wrote(*st_b["epi"])

            def emit_attT(b2):
                st2 = chs[b2]
                p2 = b2 % 2
                gT.before_write("pe")
                g_atok[p2].before_read("pe")
                K.wait_all("pe", st2["epi"])
                for ic in range(8):
                    K.pe(lambda e, ic=ic, p2=p2: e.transpose(
                        bT[:, 128 * ic:128 * (ic + 1)], att_tok[p2][:, 128 * ic:128 * (ic + 1)], ident[:, :]),
                        sig=(ic == 7))
                ev_at = ("pe", K.cnt["pe"])
                gT.wrote(ev_at)
                g_atok[p2].read(ev_at)
                g_afm.before_write("dve")
                gT.before_read("dve")
                ev_afm = K.dve(lambda e: e.tensor_copy(
                    att_fm[:, :, :], bT[:, :].rearrange("p (ic t) -> p ic t", t=128)))
                gT.read(ev_afm)
                g_afm.wrote(ev_afm)

            def emit_out2(b2):
                st2 = chs[b2]
                p2 = b2 % 2
                cmb2 = b2 % MC
                s_i = b2 // CPS
                row = (b2 % CPS) * CH
                for bk in (0, 1):
                    gP[bk].before_write("pe")
                g_afm.before_read("pe")
                if b2 == 0:
                    K.wait("pe", wt_evt["Wo"])
                    K.wait("pe", m_ready)
                    K.wait("pe", bias_ready)
                for ic in range(8):
                    for bk in (0, 1):
                        K.pe(lambda e, ic=ic, bk=bk: e.matmul(
                            bP[bk][:, :], att_fm[:, ic, :], wt["Wo"][:, ic, 512 * bk:512 * (bk + 1)],
                            start=(ic == 0), stop=False))
                for kc in range(8):
                    for bk in (0, 1):
                        K.pe(lambda e, kc=kc, bk=bk, cmb2=cmb2: e.matmul(
                            bP[bk][:, :], xn_fm[:, kc, 128 * cmb2:128 * (cmb2 + 1)],
                            m_sb[:, kc, 512 * bk:512 * (bk + 1)], start=False, stop=False))
                ev_out = [None, None]
                for bk in (0, 1):
                    K.pe(lambda e, bk=bk: e.matmul(
                        bP[bk][:, :], ones2[0:1, :], bias_hi[0:1, 512 * bk:512 * (bk + 1)],
                        start=False, stop=False))
                    ev_out[bk] = K.pe(lambda e, bk=bk: e.matmul(
                        bP[bk][:, :], ones2[0:1, :], bias_lo[0:1, 512 * bk:512 * (bk + 1)],
                        start=False, stop=True), sig=True)
                    gP[bk].wrote(ev_out[bk])
                g_xnfm.read(ev_out[1])
                g_afm.read(ev_out[1])
                st2["out"] = ev_out

            def emit_final2(b2):
                st2 = chs[b2]
                ob = out_sb[b2 % 2]
                g_obuf[b2 % 2].before_write("dve")
                ev_fin = [None, None]
                for bk in (0, 1):
                    K.wait("dve", st2["out"][bk])
                    gP[bk].before_read("dve")
                    ev_fin[bk] = K.dve(lambda e, bk=bk, b2=b2, ob=ob: e.tensor_tensor(
                        ob[:, 512 * bk:512 * (bk + 1)], bP[bk][:, :],
                        x_ch[b2 % 7][:, 512 * bk:512 * (bk + 1)], ALU.add))
                    gP[bk].read(ev_fin[bk])
                g_xbuf[b2 % 7].read(ev_fin[1])
                st2["final"] = ev_fin[1]

            def emit_store2(b2):
                st2 = chs[b2]
                s_i = b2 // CPS
                row = (b2 % CPS) * CH
                ob = out_sb[b2 % 2]
                K.wait("act", st2["final"])
                ev_st = K.dma("act", f"st{b2 % 2}",
                              lambda e, s_i=s_i, row=row, ob=ob: e.dma_start(
                                  y_d[s_i, row:row + CH, :], ob[:, :]))
                g_obuf[b2 % 2].wrote(ev_st)
                K.wait("sync", st2["final"])
                load_chunk(b2 + 7)

            def emit_A_v(c):
                p = c % 2
                cm = c % MC
                for bk in (0, 1):
                    gN[bk].before_write("pe")
                g_xnfm.before_read("pe")
                if c == 0:
                    K.wait("pe", wt_evt["Wv"])
                for kc in range(8):
                    for bk in (0, 1):
                        K.pe(lambda e, kc=kc, bk=bk, cm=cm: e.matmul(
                            bN[bk][:, :], xn_fm[:, kc, 128 * cm:128 * (cm + 1)],
                            wt["Wv"][:, kc, 512 * bk:512 * (bk + 1)],
                            start=(kc == 0), stop=(kc == 7)), sig=(kc == 7 and bk == 1))
                ev_v = ("pe", K.cnt["pe"])
                for bk in (0, 1):
                    gN[bk].wrote(ev_v)
                g_xnfm.read(ev_v)
                g_vsb[p].before_write("act")
                evs = []
                for bk in (0, 1):
                    gN[bk].before_read("act")
                    ev = K.act(lambda e, bk=bk, p=p: e.copy(
                        v_sb[p][:, :].rearrange("p (h w) -> p h w", w=66)[:, 8 * bk:8 * (bk + 1), 0:64],
                        bN[bk][:, :].rearrange("p (h w) -> p h w", w=64)))
                    gN[bk].read(ev)
                    evs.append(ev)
                g_vsb[p].wrote(*evs)

            def emit_A_pkT(c):
                p = c % 2
                cm = c % MC
                gT.before_write("pe")
                g_phi.before_read("pe")
                for ic in range(8):
                    K.pe(lambda e, ic=ic, cm=cm: e.transpose(
                        bT[:, 128 * ic:128 * (ic + 1)], phi_k[:, ic, 128 * cm:128 * (cm + 1)], ident[:, :]),
                        sig=(ic == 7))
                ev_pkt = ("pe", K.cnt["pe"])
                gT.wrote(ev_pkt)
                g_pktok[p].before_write("act")
                gT.before_read("act")
                ev_pk = K.act(lambda e, p=p: e.copy(
                    pk_tok[p][:, :].rearrange("p (ic t) -> p ic t", t=128),
                    bT[:, :].rearrange("p (ic t) -> p ic t", t=128)))
                gT.read(ev_pk)
                g_pktok[p].wrote(ev_pk)

            def emit_Z(c):
                p = c % 2
                gS[0].before_write("pe")
                g_pktok[p].before_read("pe")
                for c2 in range(8):
                    K.pe(lambda e, c2=c2, p=p: e.matmul(
                        bS[0][:, c2:c2 + 1], pk_tok[p][:, 128 * c2:128 * (c2 + 1)], onescol[:, 0:1],
                        start=(c2 == 0), stop=(c2 == 7)), sig=(c2 == 7))
                chs[c]["z"] = ("pe", K.cnt["pe"])
                gS[0].wrote(chs[c]["z"])

            def emit_Zadds(c):
                g_zb.before_write("dve")
                K.wait("dve", chs[c]["z"])
                if chs[c]["first"]:
                    ev = K.dve(lambda e: e.tensor_copy(zb[:, :], bS[0][:, 0:8]))
                else:
                    ev = K.dve(lambda e: e.tensor_tensor(zb[:, :], bS[0][:, 0:8], zb[:, :], ALU.add))
                gS[0].read(ev)
                g_zb.wrote(ev)
                chs[c]["zadd"] = ev

            def emit_SD(c, half):
                p = c % 2
                if half == 0:
                    gSD.before_write("pe")
                g_vsb[p].before_read("pe")
                lo = 64 * half
                first = chs[c]["first"]
                for c2 in range(8):
                    K.pe(lambda e, c2=c2, p=p, lo=lo, half=half, first=first: e.matmul(
                        bSD[lo:lo + 64, 64 * c2:64 * (c2 + 1)],
                        pk_tok[p][:, 128 * c2 + lo:128 * c2 + lo + 64],
                        v_sb[p][:, 66 * (2 * c2 + half):66 * (2 * c2 + half) + 64],
                        start=(c2 == 0 and first), stop=(c2 == 7 and half == 1),
                        skip_group_check=True), sig=(c2 == 7))
                ev_sd = ("pe", K.cnt["pe"])
                g_vsb[p].read(ev_sd)
                g_pktok[p].read(ev_sd)
                if half == 1:
                    gSD.wrote(ev_sd)
                    chs[c]["sd1"] = ev_sd

            def emit_shadow(c):
                p = c % 2
                g_sbf[p].before_write("act")
                gSD.before_read("act")
                K.wait("act", chs[c]["sd1"])
                K.wait("act", chs[c]["zadd"])
                evs = []
                for lo, off in ((0, 0), (64, 1)):
                    evs.append(K.act(lambda e, p=p, lo=lo, off=off: e.copy(
                        s_bf[p][lo:lo + 64, :].rearrange("p (h w) -> p h w", w=66)[:, off::2, 0:64],
                        bSD[lo:lo + 64, :].rearrange("p (q w) -> p q w", w=64))))
                    evs.append(K.act(lambda e, p=p, lo=lo, off=off: e.copy(
                        s_bf[p][lo:lo + 64, :].rearrange("p (h w) -> p h w", w=66)[:, off::2, 64],
                        zb[lo:lo + 64, :].rearrange("p (q o) -> p q o", o=1))))
                gSD.read(*evs)
                g_zb.read(evs[1], evs[3])
                g_sbf[p].wrote(evs[-1])

            NCH_ = NCH
            next_num = 0
            next_out = 0

            def emit_out_stage(b2):
                emit_attT(b2)
                emit_out2(b2)
                emit_final2(b2)
                emit_store2(b2)

            for c in range(NCH_):
                chs[c] = {"mask": {}, "numg": [None] * 4, "epi": [None] * 4,
                          "first": (c % CPS == 0)}
                if c % MC == 0:
                    m = c // MC
                    if c > 0:
                        b = c - 1
                        assert next_num == b and next_out == c - 2
                        ev_nmr = pre_nmr[m]
                        xe = [None] * MC
                        emit_attT(c - 2)
                        emit_num_group(b, 0)
                        emit_epi(b, 0)
                        xe[0] = emit_xnT(m, 0, ev_nmr)
                        emit_num_group(b, 1)
                        emit_epi(b, 1)
                        xe[1] = emit_xnT(m, 1, ev_nmr)
                        emit_out2(c - 2)
                        emit_final2(c - 2)
                        emit_store2(c - 2)
                        xe[2] = emit_xnT(m, 2, ev_nmr)
                        emit_num_group(b, 2)
                        emit_epi(b, 2)
                        emit_num_group(b, 3)
                        emit_epi(b, 3)
                        next_num = c
                        emit_attT(b)
                        emit_out2(b)
                        xe[3] = emit_xnT(m, 3, ev_nmr)
                        emit_final2(b)
                        emit_store2(b)
                        next_out = c
                    else:
                        g_stats[0].before_write("dve")
                        for cc in range(MC):
                            emit_stats_chunk(0, cc)
                        ev_nmr = emit_rstd(0)
                        for cc in range(MC):
                            emit_normalize(0, cc)
                        xe = [emit_xnT(m, cc, ev_nmr) for cc in range(MC)]
                    emit_qk(m, xe)
                    emit_A_v(c)
                    emit_A_pkT(c)
                    emit_scores(c, 0)
                    emit_mask(c, 0)
                    emit_scores(c, 1)
                    emit_mask(c, 1)
                    emit_scores(c, 2)
                    emit_mask(c, 2)
                    emit_Z(c)
                    emit_Zadds(c)
                    emit_scores(c, 3)
                    emit_mask(c, 3)
                    emit_SD(c, 0)
                    emit_SD(c, 1)
                    emit_shadow(c)
                else:
                    b = next_num
                    b2 = next_out if next_out == c - 2 else None
                    emit_scores(c, 0)
                    emit_scores(c, 1)
                    if b2 is not None:
                        emit_attT(b2)
                    emit_mask(c, 0)
                    emit_mask(c, 1)
                    emit_A_v(c)
                    if b2 is not None:
                        emit_out2(b2)
                    emit_A_pkT(c)
                    emit_num_group(b, 0)
                    emit_epi(b, 0)
                    emit_num_group(b, 1)
                    emit_Z(c)
                    emit_Zadds(c)
                    if b2 is not None:
                        emit_final2(b2)
                        emit_store2(b2)
                        next_out = b2 + 1
                    emit_epi(b, 1)
                    emit_scores(c, 2)
                    emit_mask(c, 2)
                    emit_scores(c, 3)
                    emit_mask(c, 3)
                    emit_num_group(b, 2)
                    emit_epi(b, 2)
                    emit_num_group(b, 3)
                    emit_epi(b, 3)
                    emit_SD(c, 0)
                    emit_SD(c, 1)
                    emit_shadow(c)
                    next_num = b + 1
                mm = c // MC + 1
                if mm * MC < NCH_:
                    if c % MC == 2:
                        g_stats[mm % 2].before_write("dve")
                        emit_stats_chunk(mm, 0)
                        emit_stats_chunk(mm, 1)
                        emit_stats_chunk(mm, 2)
                    elif c % MC == 3:
                        emit_stats_chunk(mm, 3)
                        emit_rstd(mm)
                        for cc in range(MC):
                            emit_normalize(mm, cc)
                if c % (4 * MC) == 4 * MC - 1:
                    K.flush()
            while next_num < NCH_:
                for grp in range(4):
                    emit_num_group(next_num, grp)
                    emit_epi(next_num, grp)
                next_num += 1
            while next_out < NCH_:
                emit_out_stage(next_out)
                next_out += 1
            K.flush()

            K.wait("sync", ("st0", K.dmac["st0"]))
            K.wait("sync", ("st1", K.dmac["st1"]))
            K.flush()

    return nc


def kernel(**inputs):
    x = np.asarray(inputs["L0_attn_x"])
    n_seq_total, L, _ = x.shape
    per = n_seq_total // N_CORES
    nc = build(per, L // MAC)
    consts = _consts()
    keys = ["L0_attn_Wq", "L0_attn_Wk", "L0_attn_Wv", "L0_attn_Wo", "L0_attn_bo",
            "L0_ffn_G1_R", "L0_ffn_G2_R", "L0_ffn_G1_I", "L0_ffn_G2_I",
            "L0_ffn_bias_real", "L0_ffn_bias_imag"]
    in_maps = []
    for core in range(N_CORES):
        m = {"L0_attn_x": np.ascontiguousarray(x[core * per:(core + 1) * per])}
        for k in keys:
            m[k] = np.asarray(inputs[k])
        m.update(consts)
        in_maps.append(m)
    res = run_bass_kernel_spmd(nc, in_maps, core_ids=list(range(N_CORES)))
    out = np.concatenate([np.asarray(r["y"]) for r in res.results], axis=0)
    return out.astype(np.float32, copy=False)


# version 26
# speedup vs baseline: 1.0430x; 1.0232x over previous
import numpy as np
import ml_dtypes

import concourse.bass as bass
import concourse.mybir as mybir
from concourse.bass_utils import run_bass_kernel_spmd

F32 = mybir.dt.float32
BF16 = mybir.dt.bfloat16
AF = mybir.ActivationFunctionType
ALU = mybir.AluOpType
AX = mybir.AxisListType
BF16NP = ml_dtypes.bfloat16

H = 16
DH = 64
E = 1024
CH = 128
MC = 4
MAC = CH * MC
EPS = 1e-5
N_CORES = 8
STAGE = 99
DBG = None


def _consts():
    c = {}
    c["c_ident"] = np.eye(128, dtype=BF16NP)
    s = np.arange(128)[:, None]
    t = np.arange(128)[None, :]
    c["c_mask"] = (s <= t).astype(BF16NP)
    rcolA = np.zeros((32, 512), np.float32)
    for b in range(32):
        rcolA[b, b * 16:(b + 1) * 16] = 1.0
    c["c_rcolA"] = rcolA.astype(BF16NP)
    rrA = np.zeros((32, 4, 128), np.float32)
    for p in range(4):
        for dg in range(128):
            rrA[8 * p + dg // 16, p, dg] = 1.0
    c["c_rrowA"] = rrA.reshape(32, 512).astype(BF16NP)
    rcolB = np.zeros((16, 512), np.float32)
    for b in range(32):
        for e in range(16):
            rcolB[e, b * 16 + e] = 1.0
    c["c_rcolB"] = rcolB.astype(BF16NP)
    rrB = np.zeros((16, 128), np.float32)
    for dg in range(128):
        rrB[dg % 16, dg] = 1.0
    c["c_rrowB"] = rrB.astype(BF16NP)
    c["c_ones2"] = np.ones((2, 128), BF16NP)
    c["c_onescol"] = np.ones((128, 1), BF16NP)
    return c


class Ctx:

    ENG = {"pe": "tensor", "dve": "vector", "act": "scalar", "sync": "sync", "gp": "gpsimd"}

    def __init__(self, nc, block, sems):
        self.nc = nc
        self.block = block
        self.sem = sems
        self.cnt = {"pe": 0, "dve": 0, "act": 0}
        self.dmac = {"wg": 0, **{f"w{i}": 0 for i in range(16)},
                     "ld0": 0, "ld1": 0, "ld2": 0, "ld3": 0, "ld4": 0, "ld5": 0, "ld6": 0,
                     "st0": 0, "st1": 0}
        self.stage = {k: [] for k in self.ENG}

    def wait(self, eng, evt):
        if evt is None:
            return
        sname, val = evt
        if val <= 0:
            return
        if sname in self.cnt:
            assert val <= self.cnt[sname], f"wait on future evt {sname}={val} (cur {self.cnt[sname]})"
        else:
            assert val <= self.dmac[sname], f"wait on future dma {sname}={val}"
        sem = self.sem[sname]
        self.stage[eng].append(lambda e, sem=sem, val=val: e.wait_ge(sem, val))

    def wait_all(self, eng, evts):
        merged = {}
        for evt in evts:
            if evt is None:
                continue
            s, v = evt
            merged[s] = max(merged.get(s, 0), v)
        for s, v in merged.items():
            self.wait(eng, (s, v))

    def pe(self, f, sig=False):
        if sig:
            self.cnt["pe"] += 1
            n = self.cnt["pe"]
            sem = self.sem["pe"]
            self.stage["pe"].append(lambda e, f=f: f(e).then_inc(sem, 1))
            return ("pe", n)
        self.stage["pe"].append(lambda e, f=f: f(e))
        return None

    def _chained(self, eng, f):
        n0 = self.cnt[eng]
        self.cnt[eng] = n0 + 1
        sem = self.sem[eng]

        def run(e, f=f, n0=n0, sem=sem):
            inst = f(e)
            if n0 > 0:
                inst._wait_ge(sem, n0)
            inst.then_inc(sem, 1)

        self.stage[eng].append(run)
        return (eng, n0 + 1)

    def dve(self, f):
        return self._chained("dve", f)

    def act(self, f):
        return self._chained("act", f)

    def dma(self, eng, ring, f):
        self.dmac[ring] += 16
        v = self.dmac[ring]
        sem = self.sem[ring]
        self.stage[eng].append(lambda e, f=f, sem=sem: f(e).then_inc(sem, 16))
        return (ring, v)

    def flush(self):
        for eng, lst in self.stage.items():
            if not lst:
                continue

            def runner(e, lst=tuple(lst)):
                for t in lst:
                    t(e)

            getattr(self.block, self.ENG[eng])(runner)
        self.stage = {k: [] for k in self.ENG}


class Guard:

    def __init__(self, ctx):
        self.ctx = ctx
        self.w = []
        self.r = []

    def before_write(self, eng):
        self.ctx.wait_all(eng, self.r if self.r else self.w)

    def wrote(self, *evts):
        self.w = [e for e in evts if e is not None]
        self.r = []

    def before_read(self, eng):
        self.ctx.wait_all(eng, self.w)

    def read(self, *evts):
        self.r += [e for e in evts if e is not None]


def build(n_seq, n_mac):
    nc = bass.Bass("TRN2", target_bir_lowering=False)
    L = n_mac * MAC

    x_d = nc.dram_tensor("L0_attn_x", [n_seq, L, E], F32, kind="ExternalInput").ap()
    y_d = nc.dram_tensor("y", [n_seq, L, E], F32, kind="ExternalOutput").ap()
    w_d = {k: nc.dram_tensor(f"L0_attn_{k}", [E, E], F32, kind="ExternalInput").ap()
           for k in ("Wq", "Wk", "Wv", "Wo")}
    bo_d = nc.dram_tensor("L0_attn_bo", [E], F32, kind="ExternalInput").ap()
    g_d = {k: nc.dram_tensor(f"L0_ffn_{k}", [1, 32, 4, 32] if k.startswith("G1") else [4, 16, 1, 16],
                             F32, kind="ExternalInput").ap()
           for k in ("G1_R", "G2_R", "G1_I", "G2_I")}
    br_d = nc.dram_tensor("L0_ffn_bias_real", [512], F32, kind="ExternalInput").ap()
    bi_d = nc.dram_tensor("L0_ffn_bias_imag", [512], F32, kind="ExternalInput").ap()
    cd = {}
    for k, arr in _consts().items():
        cd[k] = nc.dram_tensor(k, list(arr.shape), BF16 if arr.dtype == BF16NP else F32,
                               kind="ExternalInput").ap()

    import contextlib
    ctxmgr = contextlib.ExitStack()
    sb = lambda name, shape, dt: ctxmgr.enter_context(nc.sbuf_tensor(name, shape, dt))
    ps = lambda name, shape, dt: ctxmgr.enter_context(nc.psum_tensor(name, shape, dt))

    with ctxmgr:
        wt = {k: sb(f"wt_{k}", [128, 8, 1024], BF16) for k in ("Wq", "Wk", "Wv", "Wo")}
        x_ch = [sb(f"x{i}", [128, 1024], F32) for i in range(7)]
        xn_tok = sb("xn_tok", [128, 1024], BF16)
        xn_pre = sb("xn_pre", [128, 4, 1024], BF16)
        xn_fm = sb("xn_fm", [128, 8, MAC], BF16)
        phi_q = sb("phi_q", [128, 8, MAC], BF16)
        phi_k = sb("phi_k", [128, 8, MAC], BF16)
        pk_tok = [sb(f"pk_tok{i}", [128, 1024], BF16) for i in range(2)]
        v_sb = [sb(f"v_sb{i}", [128, 16 * 66], BF16) for i in range(2)]
        sc_sb = [sb(f"sc_sb{i}", [128, 2048], BF16) for i in range(2)]
        att_tok = [sb(f"att_tok{i}", [128, 1024], BF16) for i in range(2)]
        att_fm = sb("att_fm", [128, 8, 128], BF16)
        out_sb = [sb(f"out{i}", [128, 1024], F32) for i in range(2)]
        zb = sb("zb", [128, 8], F32)
        s_bf = [sb(f"s_bf{i}", [128, 16 * 66], BF16) for i in range(2)]
        den_t = sb("den_t", [128, 4], F32)
        rec_t = sb("rec_t", [128, 4], F32)
        st6 = [sb(f"st6_{i}", [128, 12], F32) for i in range(2)]
        mv4 = [sb(f"mv4_{i}", [128, 4, 2], F32) for i in range(2)]
        lnv4 = sb("lnv4", [128, 4], F32)
        rstd4 = [sb(f"rstd4_{i}", [128, 4], F32) for i in range(2)]
        nmr4 = [sb(f"nmr4_{i}", [128, 4], F32) for i in range(2)]
        epsc = sb("epsc", [128, 1], F32)
        ident = sb("ident", [128, 128], BF16)
        mask = sb("mask", [128, 128], BF16)
        rcA = sb("rcA", [32, 512], BF16)
        rrA = sb("rrA", [32, 512], BF16)
        rcB = sb("rcB", [16, 512], BF16)
        rrB = sb("rrB", [16, 128], BF16)
        ones2 = sb("ones2", [2, 128], BF16)
        onescol = sb("onescol", [128, 1], BF16)
        wstage = [sb(f"wstg{i}", [128, 8, 1024], BF16) for i in range(2)]
        m_sb = wstage[0]
        g1_sb = {p: sb(f"g1{p}", [32, 128], BF16) for p in "RI"}
        g2_sb = {p: sb(f"g2{p}", [16, 64], BF16) for p in "RI"}

        bias_hi = sb("bias_hi", [1, 1024], BF16)
        bias_lo = sb("bias_lo", [1, 1024], BF16)
        bP = [ps("bP0", [128, 512], F32), ps("bP1", [128, 512], F32)]
        bS = [ps("bS0", [128, 512], F32), ps("bS1", [128, 512], F32)]
        bN = [ps("bN0", [128, 512], F32), ps("bN1", [128, 512], F32)]
        bT = ps("bT", [128, 1024], BF16)
        bSD = ps("bSD", [128, 512], F32)

        import contextlib as _ctl
        semstack = ctxmgr
        semnames = ["pe", "dve", "act", "wg"] + [f"w{i}" for i in range(16)] + [
                    "ld0", "ld1", "ld2", "ld3", "ld4", "ld5", "ld6", "st0", "st1"]
        sems = {n: ctxmgr.enter_context(nc.semaphore(f"s_{n}")) for n in semnames}
        with nc.Block() as block:
            K = Ctx(nc, block, sems)
            gP = [Guard(K), Guard(K)]
            gS = [Guard(K), Guard(K)]
            gN = [Guard(K), Guard(K)]
            gT = Guard(K)
            g_xbuf = [Guard(K) for _ in range(7)]
            g_obuf = [Guard(K), Guard(K)]

            g_stats = [Guard(K), Guard(K)]
            g_xntok = Guard(K)
            g_xnpre = Guard(K)
            g_xnfm = Guard(K)
            g_phi = Guard(K)
            g_pktok = [Guard(K), Guard(K)]
            g_vsb = [Guard(K), Guard(K)]
            g_scsb = [[Guard(K) for _ in range(4)] for _ in range(2)]
            g_atok = [Guard(K), Guard(K)]
            g_afm = Guard(K)
            gSD = Guard(K)
            g_sbf = [Guard(K), Guard(K)]
            g_zb = Guard(K)
            bo_sb = out_sb[0]
            bcat = out_sb[1]
            tmp_f = out_sb[0]
            def accR_ap(p0):
                return x_ch[3 + p0 // 2][:, 512 * (p0 % 2):512 * (p0 % 2) + 512]

            def accI_ap(p0):
                return x_ch[5 + p0 // 2][:, 512 * (p0 % 2):512 * (p0 % 2) + 512]

            for dst, src in ((ident, cd["c_ident"]), (mask, cd["c_mask"]),
                             (rcA, cd["c_rcolA"]), (rrA, cd["c_rrowA"]),
                             (rcB, cd["c_rcolB"]), (rrB, cd["c_rrowB"]),
                             (ones2, cd["c_ones2"]), (onescol, cd["c_onescol"])):
                K.dma("gp", "wg", lambda e, d=dst, s=src: e.dma_start(d[:, :], s))
            K.dma("gp", "wg", lambda e: e.dma_start(bo_sb[0:1, 0:1024], bo_d.rearrange("(o k) -> o k", o=1)))
            K.dma("gp", "wg", lambda e: e.dma_start(bcat[0:1, 0:512], br_d.rearrange("(o k) -> o k", o=1)))
            K.dma("gp", "wg", lambda e: e.dma_start(bcat[0:1, 512:1024], bi_d.rearrange("(o k) -> o k", o=1)))
            for p in "RI":
                K.dma("gp", "wg",
                      lambda e, p=p: e.dma_start(g1_sb[p][:, :], g_d[f"G1_{p}"].rearrange("o b c d -> (o b) (c d)")))
                K.dma("gp", "wg",
                      lambda e, p=p: e.dma_start(g2_sb[p][:, :].rearrange("p (c g) -> p c g", g=16),
                                                 g_d[f"G2_{p}"].rearrange("c e o g -> (e o) c g")))
            small_done = ("wg", K.dmac["wg"])
            w_evt = {}

            def load_weight(k, idx, war_evt=None):
                K.wait("gp", war_evt)
                evs = []
                for q in range(4):
                    ring = f"w{4 * idx + q}"
                    K.dma("gp", ring,
                          lambda e, k=k, idx=idx, q=q: e.dma_start(
                              wstage[idx % 2][:, 2 * q:2 * q + 2, :],
                              w_d[k].rearrange("(r p) k -> p r k", p=128)[:, 2 * q:2 * q + 2, :]))
                    evs.append((ring, K.dmac[ring]))
                w_evt[k] = evs

            load_weight("Wq", 0)
            load_weight("Wk", 1)

            NCH = n_seq * n_mac * MC
            x_evt = {}

            def load_chunk(j):
                if j >= NCH:
                    return
                s_i = j // (n_mac * MC)
                row = (j % (n_mac * MC)) * CH
                ring = f"ld{j % 7}"
                g_xbuf[j % 7].before_write("sync")
                K.dma("sync", ring,
                      lambda e, s_i=s_i, row=row: e.dma_start(
                          x_ch[j % 7][:, :], x_d[s_i, row:row + CH, :]))
                x_evt[j] = (ring, K.dmac[ring])
                g_xbuf[j % 7].wrote(x_evt[j])

            K.wait("dve", small_done)
            for _vb in v_sb:
                K.dve(lambda e, _vb=_vb: e.memset(_vb[:, :].rearrange("p (h w) -> p h w", w=66)[:, :, 64:65], 1.0))
                K.dve(lambda e, _vb=_vb: e.memset(_vb[:, :].rearrange("p (h w) -> p h w", w=66)[:, :, 65:66], 0.0))
            K.dve(lambda e: e.memset(zb[:, :], 0.0))
            for _sbf in s_bf:
                K.dve(lambda e, _sbf=_sbf: e.memset(_sbf[:, :], 0.0))
            K.dve(lambda e: e.memset(epsc[:, :], EPS))
            evb1 = K.dve(lambda e: e.tensor_add(bcat[0:1, 0:512], bcat[0:1, 0:512], bo_sb[0:1, 0:512]))
            evb2 = K.dve(lambda e: e.tensor_add(bcat[0:1, 512:1024], bcat[0:1, 512:1024], bo_sb[0:1, 512:1024]))
            evb3 = K.dve(lambda e: e.tensor_copy(bias_hi[0:1, :], bcat[0:1, 0:1024]))
            evb4 = K.dve(lambda e: e.tensor_sub(bias_lo[0:1, :], bcat[0:1, 0:1024], bias_hi[0:1, :]))
            bias_ready = evb4


            wt_evt = {}

            wprep_pe = {}

            def weight_prep(k, idx):
                stg = wstage[idx % 2]
                K.wait_all("pe", w_evt[k])
                last_evac = None
                for r in range(8):
                    gT.before_write("pe")
                    for kk in range(8):
                        K.pe(lambda e, r=r, kk=kk, stg=stg: e.transpose(
                            bT[:, 128 * kk:128 * (kk + 1)], stg[:, r, 128 * kk:128 * (kk + 1)], ident[:, :]),
                            sig=(kk == 7))
                    evt_t = ("pe", K.cnt["pe"])
                    gT.wrote(evt_t)
                    K.wait("act", evt_t)
                    ev = K.act(lambda e, r=r, k=k: e.copy(
                        wt[k][:, :, 128 * r:128 * (r + 1)],
                        bT[:, :].rearrange("p (kk t) -> p kk t", t=128)))
                    gT.read(ev)
                    last_evac = ev
                wprep_pe[k] = ("pe", K.cnt["pe"])
                wt_evt[k] = last_evac

            K.wait("pe", small_done)
            for j in range(3):
                load_chunk(j)
            weight_prep("Wq", 0)
            load_weight("Wv", 2, war_evt=wprep_pe["Wq"])
            weight_prep("Wk", 1)
            load_weight("Wo", 3, war_evt=wprep_pe["Wk"])
            weight_prep("Wv", 2)
            weight_prep("Wo", 3)

            K.wait("pe", small_done)
            acc = {"R": accR_ap, "I": accI_ap}
            accev = {}
            g_t1s, g_t2s = Guard(K), Guard(K)
            t1s = xn_tok[0:32, 0:512]
            t2s = xn_tok[0:16, 512:1024]
            g_aS = g_afm
            aS = att_fm[:, :, :].rearrange("p a b -> p (a b)")[:, 0:512]
            for p in "RI":
                for c in range(4):
                    gS[0].before_write("pe")
                    e1 = K.pe(lambda e, p=p, c=c: e.matmul(
                        bS[0][0:32, :], g1_sb[p][0:32, 32 * c:32 * (c + 1)], rcA[0:32, :],
                        start=True, stop=True), sig=True)
                    gS[0].wrote(e1)
                    gS[0].before_read("act")
                    g_t1s.before_write("act")
                    ev1 = K.act(lambda e: e.copy(t1s[0:32, :], bS[0][0:32, :]))
                    g_t1s.wrote(ev1)
                    gS[0].read(ev1)
                    gS[1].before_write("pe")
                    e2 = K.pe(lambda e, p=p, c=c: e.matmul(
                        bS[1][0:16, :], g2_sb[p][0:16, 16 * c:16 * (c + 1)], rcB[0:16, :],
                        start=True, stop=True), sig=True)
                    gS[1].wrote(e2)
                    gS[1].before_read("act")
                    g_t2s.before_write("act")
                    ev2 = K.act(lambda e: e.copy(t2s[0:16, :], bS[1][0:16, :]))
                    g_t2s.wrote(ev2)
                    gS[1].read(ev2)
                    for p0 in range(4):
                        gN[0].before_write("pe")
                        K.wait("pe", ev1)
                        eA = K.pe(lambda e, p0=p0: e.matmul(
                            bN[0][:, :], rrA[0:32, 128 * p0:128 * (p0 + 1)], t1s[0:32, :],
                            start=True, stop=True), sig=True)
                        gN[0].wrote(eA)
                        gSD.before_write("pe")
                        K.wait("pe", ev2)
                        eB = K.pe(lambda e: e.matmul(
                            bSD[:, :], rrB[0:16, :], t2s[0:16, :],
                            start=True, stop=True), sig=True)
                        gSD.wrote(eB)
                        gN[0].before_read("act")
                        g_aS.before_write("act")
                        evA = K.act(lambda e: e.copy(aS[:, :], bN[0][:, :]))
                        g_aS.wrote(evA)
                        gN[0].read(evA)
                        gSD.before_read("dve")
                        K.wait("dve", evA)
                        if c == 0:
                            evP = K.dve(lambda e, p=p, p0=p0: e.tensor_mul(
                                acc[p](p0), bSD[:, :], aS[:, :]))
                        else:
                            K.dve(lambda e: e.tensor_mul(tmp_f[:, 0:512], bSD[:, :], aS[:, :]))
                            evP = K.dve(lambda e, p=p, p0=p0: e.tensor_add(
                                acc[p](p0), acc[p](p0), tmp_f[:, 0:512]))
                        gSD.read(evP)
                        g_aS.read(evP)
                        g_t1s.read(eA)
                        g_t2s.read(eB)
                accev[p] = evP
            m_parts = []
            K.wait("dve", wprep_pe["Wv"])
            for p0 in range(4):
                m_parts.append(K.dve(lambda e, p0=p0: e.tensor_copy(m_sb[:, p0, 0:512], accR_ap(p0))))
                m_parts.append(K.dve(lambda e, p0=p0: e.tensor_copy(m_sb[:, p0, 512:1024], accI_ap(p0))))
                m_parts.append(K.dve(lambda e, p0=p0: e.tensor_scalar(
                    m_sb[:, 4 + p0, 0:512], accI_ap(p0), -1.0, None, ALU.mult)))
                m_parts.append(K.dve(lambda e, p0=p0: e.tensor_copy(m_sb[:, 4 + p0, 512:1024], accR_ap(p0))))
            m_ready = m_parts[-1]

            K.wait("sync", m_ready)
            for j in range(3, 7):
                load_chunk(j)
            K.flush()

            chs = {}
            CPS = n_mac * MC

            pre_nmr = {}

            def emit_stats_chunk(mm, cc):
                s = mm % 2
                j = mm * MC + cc
                K.wait("dve", x_evt[j])
                K.dve(lambda e, j=j, s=s: e.bn_stats(st6[s][:, 0:6], x_ch[j % 7][:, 0:512]))
                K.dve(lambda e, j=j, s=s: e.bn_stats(st6[s][:, 6:12], x_ch[j % 7][:, 512:1024]))
                ev = K.dve(lambda e, cc=cc, s=s: e.bn_aggr(mv4[s][:, cc, 0:2], st6[s][:, :]))
                g_xbuf[j % 7].read(ev)

            def emit_rstd(mm):
                s = mm % 2
                ev_mv = ("dve", K.cnt["dve"])
                K.wait("act", ev_mv)
                K.act(lambda e, s=s: e.activation(lnv4[:, 0:4], mv4[s][:, :, 1], AF.Ln,
                                                  bias=epsc[:, 0:1], scale=1.0))
                ev_rstd = K.act(lambda e, s=s: e.activation(rstd4[s][:, 0:4], lnv4[:, 0:4], AF.Exp,
                                                            bias=0.0, scale=-0.5))
                K.wait("dve", ev_rstd)
                ev_nmr = K.dve(lambda e, s=s: e.scalar_tensor_tensor(
                    nmr4[s][:, 0:4], mv4[s][:, :, 0], -1.0, rstd4[s][:, 0:4], ALU.mult, ALU.mult))
                g_stats[s].wrote(ev_nmr, ev_rstd)
                pre_nmr[mm] = ev_nmr
                return ev_nmr

            nrm_ev = {}

            def emit_normalize(m, cc):
                s = m % 2
                j = m * MC + cc
                if cc == 0:
                    g_xnpre.before_write("act")
                    K.wait("act", pre_nmr[m])
                ev_xn = K.act(lambda e, j=j, cc=cc, s=s: e.activation(
                    xn_pre[:, cc, :], x_ch[j % 7][:, :], AF.Identity,
                    bias=nmr4[s][:, cc:cc + 1], scale=rstd4[s][:, cc:cc + 1]))
                g_stats[s].read(ev_xn)
                g_xbuf[j % 7].read(ev_xn)
                nrm_ev[(m, cc)] = ev_xn
                if cc == MC - 1:
                    g_xnpre.wrote(*[nrm_ev[(m, k)] for k in range(MC)])

            def emit_xnT(m, cc, ev_nmr):
                gT.before_write("pe")
                K.wait("pe", nrm_ev[(m, cc)])
                for ic in range(8):
                    K.pe(lambda e, ic=ic, cc=cc: e.transpose(
                        bT[:, 128 * ic:128 * (ic + 1)], xn_pre[:, cc, 128 * ic:128 * (ic + 1)], ident[:, :]),
                        sig=(ic == 7))
                ev_t = ("pe", K.cnt["pe"])
                gT.wrote(ev_t)
                g_xnpre.read(ev_t)
                gT.before_read("dve")
                if cc == 0:
                    g_xnfm.before_write("dve")
                ev_evac = K.dve(lambda e, cc=cc: e.tensor_copy(
                    xn_fm[:, :, 128 * cc:128 * (cc + 1)],
                    bT[:, :].rearrange("p (kk t) -> p kk t", t=128)))
                gT.read(ev_evac)
                return ev_evac

            def emit_qk(m, xn_evts):
                g_xnfm.wrote(*xn_evts)
                if m == 0:
                    K.wait("pe", wt_evt["Wq"])
                qk_banks = [bN[0], bN[1], bP[0], bP[1]]
                qk_guards = [gN[0], gN[1], gP[0], gP[1]]
                esc_slots = [xn_tok[:, 0:512], xn_tok[:, 512:1024],
                             att_tok[0][:, 0:512], att_tok[0][:, 512:1024]]
                esc_guards = [g_xntok, g_xntok, g_atok[0], g_atok[0]]
                for pi, (wkey, phi_dst) in enumerate((("Wq", phi_q), ("Wk", phi_k))):
                    if m == 0 and pi == 1:
                        K.wait("pe", wt_evt["Wk"])
                    for i in range(8):
                        r = (pi * 8 + i) % 4
                        bank = qk_guards[r]
                        nb = qk_banks[r]
                        bank.before_write("pe")
                        if i == 0 and pi == 0:
                            g_xnfm.before_read("pe")
                            g_phi.before_write("dve")
                        for kc in range(8):
                            K.pe(lambda e, i=i, kc=kc, wkey=wkey, nb=nb: e.matmul(
                                nb[:, :], wt[wkey][:, kc, 128 * i:128 * (i + 1)],
                                xn_fm[:, kc, :], start=(kc == 0), stop=(kc == 7)), sig=(kc == 7))
                        ev_q = ("pe", K.cnt["pe"])
                        bank.wrote(ev_q)
                        esc = esc_slots[r]
                        eg = esc_guards[r]
                        eg.before_write("act")
                        bank.before_read("act")
                        ev_e = K.act(lambda e, esc=esc, nb=nb: e.activation(esc, nb[:, :], AF.Exp))
                        eg.wrote(ev_e)
                        K.wait("dve", ev_e)
                        K.dve(lambda e, esc=esc: e.tensor_scalar(esc, esc, 1.0, None, ALU.min))
                        ev_phi = K.dve(lambda e, i=i, esc=esc, nb=nb, phi_dst=phi_dst: e.scalar_tensor_tensor(
                            phi_dst[:, i, :], nb[:, :], 0.0, esc, ALU.max, ALU.add))
                        bank.read(ev_e, ev_phi)
                        eg.read(ev_phi)
                g_phi.wrote(("dve", K.cnt["dve"]))

            sc_pos = {}
            score_sets = [(1, [0, 2, 4, 6]), (0, [8, 10, 12, 14]),
                          (1, [1, 3, 5, 7]), (0, [9, 11, 13, 15])]
            for _si, (_b, _hs) in enumerate(score_sets):
                for _j, _h in enumerate(_hs):
                    sc_pos[_h] = 4 * _si + _j
            set_of_head = {h: si for si, (_b, hs) in enumerate(score_sets) for h in hs}

            def emit_scores(c, set_i):
                p = c % 2
                cm = c % MC
                bno, heads = score_sets[set_i]
                bank = gS[bno]
                bank.before_write("pe")
                g_phi.before_read("pe")
                for j, h in enumerate(heads):
                    K.pe(lambda e, h=h, j=j, cm=cm, bno=bno: e.matmul(
                        bS[bno][:, 128 * j:128 * (j + 1)],
                        phi_k[64 * (h % 2):64 * (h % 2) + 64, h // 2, 128 * cm:128 * (cm + 1)],
                        phi_q[64 * (h % 2):64 * (h % 2) + 64, h // 2, 128 * cm:128 * (cm + 1)],
                        start=True, stop=True), sig=(j == 3))
                ev_sc = ("pe", K.cnt["pe"])
                bank.wrote(ev_sc)

            def emit_mask(c, set_i):
                p = c % 2
                bno, heads = score_sets[set_i]
                bank = gS[bno]
                g_scsb[p][set_i].before_write("dve")
                bank.before_read("dve")
                ev = K.dve(lambda e, bno=bno, set_i=set_i, p=p: e.tensor_tensor(
                    sc_sb[p][:, 512 * set_i:512 * (set_i + 1)].rearrange("p (j t) -> p j t", t=128),
                    bS[bno][:, :].rearrange("p (j t) -> p j t", t=128),
                    mask[:, :].rearrange("p (o t) -> p o t", o=1).broadcast_to((128, 4, 128)),
                    ALU.mult))
                bank.read(ev)
                g_scsb[p][set_i].wrote(ev)
                for h in heads:
                    chs[c]["mask"][h] = ev

            def emit_num_group(b, grp):
                st_b = chs[b]
                pb = b % 2
                cmb = b % MC
                bank = gN[grp % 2]
                nb = bN[grp % 2]
                bank.before_write("pe")
                K.wait_all("pe", [st_b["mask"][4 * grp + j] for j in range(4)])
                if grp == 0 and not st_b["first"]:
                    g_sbf[(b - 1) % 2].before_read("pe")
                if not st_b["first"]:
                    g_phi.before_read("pe")
                for j in range(4):
                    h = 4 * grp + j
                    K.pe(lambda e, h=h, j=j, pb=pb, nb=nb: e.matmul(
                        nb[:, 66 * j:66 * j + 65],
                        sc_sb[pb][:, 128 * sc_pos[h]:128 * (sc_pos[h] + 1)],
                        v_sb[pb][:, 66 * h:66 * h + 65],
                        start=(j == 0), stop=(st_b["first"] and j == 3)),
                        sig=(st_b["first"] and j == 3))
                if not st_b["first"]:
                    for j in range(4):
                        h = 4 * grp + j
                        K.pe(lambda e, h=h, j=j, cmb=cmb, b=b, nb=nb: e.matmul(
                            nb[:, 66 * j:66 * j + 65],
                            phi_q[:, h // 2, 128 * cmb:128 * (cmb + 1)],
                            s_bf[(b - 1) % 2][:, 66 * h:66 * h + 65],
                            start=False, stop=(j == 3)), sig=(j == 3))
                ev_g = ("pe", K.cnt["pe"])
                bank.wrote(ev_g)
                st_b["numg"][grp] = ev_g
                g_scsb[pb][set_of_head[4 * grp]].read(ev_g)
                g_scsb[pb][set_of_head[4 * grp + 1]].read(ev_g)
                g_vsb[pb].read(ev_g)
                if not st_b["first"]:
                    g_sbf[(b - 1) % 2].read(ev_g)
                    g_phi.read(ev_g)

            def emit_epi(b, grp):
                st_b = chs[b]
                pb = b % 2
                bank = gN[grp % 2]
                nb = bN[grp % 2]
                bank.before_read("dve")
                if grp == 0:
                    g_atok[pb].before_write("dve")
                K.dve(lambda e, nb=nb: e.tensor_scalar(
                    den_t[:, 0:4],
                    nb[:, 0:264].rearrange("p (j w) -> p j w", w=66)[:, :, 64],
                    EPS, None, ALU.add))
                K.dve(lambda e: e.reciprocal(rec_t[:, 0:4], den_t[:, 0:4]))
                ev_att = K.dve(lambda e, grp=grp, pb=pb, nb=nb: e.tensor_tensor(
                    att_tok[pb][:, 256 * grp:256 * (grp + 1)].rearrange("p (j w) -> p j w", w=64),
                    nb[:, 0:264].rearrange("p (j w) -> p j w", w=66)[:, :, 0:64],
                    rec_t[:, 0:4].rearrange("p (j o) -> p j o", o=1).broadcast_to((128, 4, 64)),
                    ALU.mult))
                bank.read(ev_att)
                st_b["epi"][grp] = ev_att
                if grp == 3:
                    g_atok[pb].wrote(*st_b["epi"])

            def emit_attT(b2):
                st2 = chs[b2]
                p2 = b2 % 2
                gT.before_write("pe")
                g_atok[p2].before_read("pe")
                K.wait_all("pe", st2["epi"])
                for ic in range(8):
                    K.pe(lambda e, ic=ic, p2=p2: e.transpose(
                        bT[:, 128 * ic:128 * (ic + 1)], att_tok[p2][:, 128 * ic:128 * (ic + 1)], ident[:, :]),
                        sig=(ic == 7))
                ev_at = ("pe", K.cnt["pe"])
                gT.wrote(ev_at)
                g_atok[p2].read(ev_at)
                g_afm.before_write("dve")
                gT.before_read("dve")
                ev_afm = K.dve(lambda e: e.tensor_copy(
                    att_fm[:, :, :], bT[:, :].rearrange("p (ic t) -> p ic t", t=128)))
                gT.read(ev_afm)
                g_afm.wrote(ev_afm)

            def emit_out2(b2):
                st2 = chs[b2]
                p2 = b2 % 2
                cmb2 = b2 % MC
                s_i = b2 // CPS
                row = (b2 % CPS) * CH
                for bk in (0, 1):
                    gP[bk].before_write("pe")
                g_afm.before_read("pe")
                if b2 == 0:
                    K.wait("pe", wt_evt["Wo"])
                    K.wait("pe", m_ready)
                    K.wait("pe", bias_ready)
                for ic in range(8):
                    for bk in (0, 1):
                        K.pe(lambda e, ic=ic, bk=bk: e.matmul(
                            bP[bk][:, :], att_fm[:, ic, :], wt["Wo"][:, ic, 512 * bk:512 * (bk + 1)],
                            start=(ic == 0), stop=False))
                for kc in range(8):
                    for bk in (0, 1):
                        K.pe(lambda e, kc=kc, bk=bk, cmb2=cmb2: e.matmul(
                            bP[bk][:, :], xn_fm[:, kc, 128 * cmb2:128 * (cmb2 + 1)],
                            m_sb[:, kc, 512 * bk:512 * (bk + 1)], start=False, stop=False))
                ev_out = [None, None]
                for bk in (0, 1):
                    K.pe(lambda e, bk=bk: e.matmul(
                        bP[bk][:, :], ones2[0:1, :], bias_hi[0:1, 512 * bk:512 * (bk + 1)],
                        start=False, stop=False))
                    ev_out[bk] = K.pe(lambda e, bk=bk: e.matmul(
                        bP[bk][:, :], ones2[0:1, :], bias_lo[0:1, 512 * bk:512 * (bk + 1)],
                        start=False, stop=True), sig=True)
                    gP[bk].wrote(ev_out[bk])
                g_xnfm.read(ev_out[1])
                g_afm.read(ev_out[1])
                st2["out"] = ev_out

            def emit_final2(b2):
                st2 = chs[b2]
                ob = out_sb[b2 % 2]
                g_obuf[b2 % 2].before_write("dve")
                ev_fin = [None, None]
                for bk in (0, 1):
                    K.wait("dve", st2["out"][bk])
                    gP[bk].before_read("dve")
                    ev_fin[bk] = K.dve(lambda e, bk=bk, b2=b2, ob=ob: e.tensor_tensor(
                        ob[:, 512 * bk:512 * (bk + 1)], bP[bk][:, :],
                        x_ch[b2 % 7][:, 512 * bk:512 * (bk + 1)], ALU.add))
                    gP[bk].read(ev_fin[bk])
                g_xbuf[b2 % 7].read(ev_fin[1])
                st2["final"] = ev_fin[1]

            def emit_store2(b2):
                st2 = chs[b2]
                s_i = b2 // CPS
                row = (b2 % CPS) * CH
                ob = out_sb[b2 % 2]
                K.wait("act", st2["final"])
                ev_st = K.dma("act", f"st{b2 % 2}",
                              lambda e, s_i=s_i, row=row, ob=ob: e.dma_start(
                                  y_d[s_i, row:row + CH, :], ob[:, :]))
                g_obuf[b2 % 2].wrote(ev_st)
                K.wait("sync", st2["final"])
                load_chunk(b2 + 7)

            def emit_A_v(c):
                p = c % 2
                cm = c % MC
                for bk in (0, 1):
                    gN[bk].before_write("pe")
                g_xnfm.before_read("pe")
                if c == 0:
                    K.wait("pe", wt_evt["Wv"])
                for kc in range(8):
                    for bk in (0, 1):
                        K.pe(lambda e, kc=kc, bk=bk, cm=cm: e.matmul(
                            bN[bk][:, :], xn_fm[:, kc, 128 * cm:128 * (cm + 1)],
                            wt["Wv"][:, kc, 512 * bk:512 * (bk + 1)],
                            start=(kc == 0), stop=(kc == 7)), sig=(kc == 7 and bk == 1))
                ev_v = ("pe", K.cnt["pe"])
                for bk in (0, 1):
                    gN[bk].wrote(ev_v)
                g_xnfm.read(ev_v)
                g_vsb[p].before_write("act")
                evs = []
                for bk in (0, 1):
                    gN[bk].before_read("act")
                    ev = K.act(lambda e, bk=bk, p=p: e.copy(
                        v_sb[p][:, :].rearrange("p (h w) -> p h w", w=66)[:, 8 * bk:8 * (bk + 1), 0:64],
                        bN[bk][:, :].rearrange("p (h w) -> p h w", w=64)))
                    gN[bk].read(ev)
                    evs.append(ev)
                g_vsb[p].wrote(*evs)

            def emit_A_pkT(c):
                p = c % 2
                cm = c % MC
                gT.before_write("pe")
                g_phi.before_read("pe")
                for ic in range(8):
                    K.pe(lambda e, ic=ic, cm=cm: e.transpose(
                        bT[:, 128 * ic:128 * (ic + 1)], phi_k[:, ic, 128 * cm:128 * (cm + 1)], ident[:, :]),
                        sig=(ic == 7))
                ev_pkt = ("pe", K.cnt["pe"])
                gT.wrote(ev_pkt)
                g_pktok[p].before_write("act")
                gT.before_read("act")
                ev_pk = K.act(lambda e, p=p: e.copy(
                    pk_tok[p][:, :].rearrange("p (ic t) -> p ic t", t=128),
                    bT[:, :].rearrange("p (ic t) -> p ic t", t=128)))
                gT.read(ev_pk)
                g_pktok[p].wrote(ev_pk)

            def emit_Z(c):
                p = c % 2
                gS[0].before_write("pe")
                g_pktok[p].before_read("pe")
                for c2 in range(8):
                    K.pe(lambda e, c2=c2, p=p: e.matmul(
                        bS[0][:, c2:c2 + 1], pk_tok[p][:, 128 * c2:128 * (c2 + 1)], onescol[:, 0:1],
                        start=(c2 == 0), stop=(c2 == 7)), sig=(c2 == 7))
                chs[c]["z"] = ("pe", K.cnt["pe"])
                gS[0].wrote(chs[c]["z"])

            def emit_Zadds(c):
                g_zb.before_write("dve")
                K.wait("dve", chs[c]["z"])
                if chs[c]["first"]:
                    ev = K.dve(lambda e: e.tensor_copy(zb[:, :], bS[0][:, 0:8]))
                else:
                    ev = K.dve(lambda e: e.tensor_tensor(zb[:, :], bS[0][:, 0:8], zb[:, :], ALU.add))
                gS[0].read(ev)
                g_zb.wrote(ev)
                chs[c]["zadd"] = ev

            def emit_SD(c, half):
                p = c % 2
                if half == 0:
                    gSD.before_write("pe")
                g_vsb[p].before_read("pe")
                lo = 64 * half
                first = chs[c]["first"]
                for c2 in range(8):
                    K.pe(lambda e, c2=c2, p=p, lo=lo, half=half, first=first: e.matmul(
                        bSD[lo:lo + 64, 64 * c2:64 * (c2 + 1)],
                        pk_tok[p][:, 128 * c2 + lo:128 * c2 + lo + 64],
                        v_sb[p][:, 66 * (2 * c2 + half):66 * (2 * c2 + half) + 64],
                        start=(c2 == 0 and first), stop=(c2 == 7 and half == 1),
                        skip_group_check=True), sig=(c2 == 7))
                ev_sd = ("pe", K.cnt["pe"])
                g_vsb[p].read(ev_sd)
                g_pktok[p].read(ev_sd)
                if half == 1:
                    gSD.wrote(ev_sd)
                    chs[c]["sd1"] = ev_sd

            def emit_shadow(c):
                p = c % 2
                g_sbf[p].before_write("act")
                gSD.before_read("act")
                K.wait("act", chs[c]["sd1"])
                K.wait("act", chs[c]["zadd"])
                evs = []
                for lo, off in ((0, 0), (64, 1)):
                    evs.append(K.act(lambda e, p=p, lo=lo, off=off: e.copy(
                        s_bf[p][lo:lo + 64, :].rearrange("p (h w) -> p h w", w=66)[:, off::2, 0:64],
                        bSD[lo:lo + 64, :].rearrange("p (q w) -> p q w", w=64))))
                    evs.append(K.act(lambda e, p=p, lo=lo, off=off: e.copy(
                        s_bf[p][lo:lo + 64, :].rearrange("p (h w) -> p h w", w=66)[:, off::2, 64],
                        zb[lo:lo + 64, :].rearrange("p (q o) -> p q o", o=1))))
                gSD.read(*evs)
                g_zb.read(evs[1], evs[3])
                g_sbf[p].wrote(evs[-1])

            NCH_ = NCH
            next_num = 0
            next_out = 0

            def emit_out_stage(b2):
                emit_attT(b2)
                emit_out2(b2)
                emit_final2(b2)
                emit_store2(b2)

            for c in range(NCH_):
                chs[c] = {"mask": {}, "numg": [None] * 4, "epi": [None] * 4,
                          "first": (c % CPS == 0)}
                if c % MC == 0:
                    m = c // MC
                    if c > 0:
                        b = c - 1
                        assert next_num == b and next_out == c - 2
                        ev_nmr = pre_nmr[m]
                        xe = [None] * MC
                        emit_attT(c - 2)
                        emit_num_group(b, 0)
                        emit_epi(b, 0)
                        xe[0] = emit_xnT(m, 0, ev_nmr)
                        emit_num_group(b, 1)
                        emit_epi(b, 1)
                        xe[1] = emit_xnT(m, 1, ev_nmr)
                        emit_out2(c - 2)
                        emit_final2(c - 2)
                        emit_store2(c - 2)
                        xe[2] = emit_xnT(m, 2, ev_nmr)
                        emit_num_group(b, 2)
                        emit_epi(b, 2)
                        emit_num_group(b, 3)
                        emit_epi(b, 3)
                        next_num = c
                        emit_attT(b)
                        emit_out2(b)
                        xe[3] = emit_xnT(m, 3, ev_nmr)
                        emit_final2(b)
                        emit_store2(b)
                        next_out = c
                    else:
                        g_stats[0].before_write("dve")
                        for cc in range(MC):
                            emit_stats_chunk(0, cc)
                        ev_nmr = emit_rstd(0)
                        for cc in range(MC):
                            emit_normalize(0, cc)
                        xe = [emit_xnT(m, cc, ev_nmr) for cc in range(MC)]
                    emit_qk(m, xe)
                    emit_A_v(c)
                    emit_A_pkT(c)
                    emit_scores(c, 0)
                    emit_mask(c, 0)
                    emit_scores(c, 1)
                    emit_mask(c, 1)
                    emit_scores(c, 2)
                    emit_mask(c, 2)
                    emit_scores(c, 3)
                    emit_mask(c, 3)
                    emit_Z(c)
                    emit_Zadds(c)
                    emit_SD(c, 0)
                    emit_SD(c, 1)
                    emit_shadow(c)
                else:
                    b = next_num
                    b2 = next_out if next_out == c - 2 else None
                    emit_scores(c, 0)
                    emit_scores(c, 1)
                    if b2 is not None:
                        emit_attT(b2)
                    emit_mask(c, 0)
                    emit_mask(c, 1)
                    emit_A_v(c)
                    if b2 is not None:
                        emit_out2(b2)
                    emit_A_pkT(c)
                    emit_num_group(b, 0)
                    emit_epi(b, 0)
                    emit_num_group(b, 1)
                    emit_Z(c)
                    emit_Zadds(c)
                    if b2 is not None:
                        emit_final2(b2)
                        emit_store2(b2)
                        next_out = b2 + 1
                    emit_epi(b, 1)
                    emit_scores(c, 2)
                    emit_mask(c, 2)
                    emit_scores(c, 3)
                    emit_mask(c, 3)
                    emit_num_group(b, 2)
                    emit_epi(b, 2)
                    emit_num_group(b, 3)
                    emit_epi(b, 3)
                    emit_SD(c, 0)
                    emit_SD(c, 1)
                    mm = c // MC + 1
                    if mm * MC < NCH_:
                        if c % MC == 2:
                            g_stats[mm % 2].before_write("dve")
                            emit_stats_chunk(mm, 0)
                            emit_stats_chunk(mm, 1)
                            emit_stats_chunk(mm, 2)
                        elif c % MC == 3:
                            emit_stats_chunk(mm, 3)
                            emit_rstd(mm)
                            for cc in range(MC):
                                emit_normalize(mm, cc)
                    emit_shadow(c)
                    next_num = b + 1
                if c % (4 * MC) == 4 * MC - 1:
                    K.flush()
            while next_num < NCH_:
                for grp in range(4):
                    emit_num_group(next_num, grp)
                    emit_epi(next_num, grp)
                next_num += 1
            while next_out < NCH_:
                emit_out_stage(next_out)
                next_out += 1
            K.flush()

            K.wait("sync", ("st0", K.dmac["st0"]))
            K.wait("sync", ("st1", K.dmac["st1"]))
            K.flush()

    return nc


def kernel(**inputs):
    x = np.asarray(inputs["L0_attn_x"])
    n_seq_total, L, _ = x.shape
    per = n_seq_total // N_CORES
    nc = build(per, L // MAC)
    consts = _consts()
    keys = ["L0_attn_Wq", "L0_attn_Wk", "L0_attn_Wv", "L0_attn_Wo", "L0_attn_bo",
            "L0_ffn_G1_R", "L0_ffn_G2_R", "L0_ffn_G1_I", "L0_ffn_G2_I",
            "L0_ffn_bias_real", "L0_ffn_bias_imag"]
    in_maps = []
    for core in range(N_CORES):
        m = {"L0_attn_x": np.ascontiguousarray(x[core * per:(core + 1) * per])}
        for k in keys:
            m[k] = np.asarray(inputs[k])
        m.update(consts)
        in_maps.append(m)
    res = run_bass_kernel_spmd(nc, in_maps, core_ids=list(range(N_CORES)))
    out = np.concatenate([np.asarray(r["y"]) for r in res.results], axis=0)
    return out.astype(np.float32, copy=False)
